# version 22
# baseline (speedup 1.0000x reference)
import math
import numpy as np
import concourse.bass as bass
import concourse.mybir as mybir
from concourse.alu_op_type import AluOpType as OP
from concourse.bass_utils import run_bass_kernel_spmd

F32 = mybir.dt.float32
BF16 = mybir.dt.bfloat16
AF = mybir.ActivationFunctionType
AX = mybir.AxisListType

N, TC, D, NF, H, KH, M, FF, E, FEXP, FS = 2048, 16, 1024, 8, 16, 64, 256, 4096, 8, 512, 1024
EPS0, EPS1 = 1e-5, 1e-6
SCALE = float(KH) ** (-0.25)
LNRTM = math.log(math.sqrt(M))
SQC = 0.5 * SCALE * SCALE


class SB:
    ALIGN = 64

    def __init__(self, nc, nbytes):
        self.nc = nc
        self.arena = nc.alloc_sbuf_tensor("arena", [128, nbytes], mybir.dt.uint8)
        self.base = nc.lookup_mloc(self.arena).addr
        self.freel = [[0, nbytes]]
        self.live = {}
        self.peak = 0

    def alloc(self, name, shape, dtype):
        import functools, operator
        per = functools.reduce(operator.mul, list(shape[1:]), 1) * mybir.dt.size(dtype)
        size = (per + self.ALIGN - 1) // self.ALIGN * self.ALIGN
        for ent in self.freel:
            if ent[1] >= size:
                st = ent[0]
                ent[0] += size
                ent[1] -= size
                h = self.nc.virtual_sbuf_tensor(name, list(shape), dtype)
                self.nc._tensor(h.name, list(shape), dtype, type="SB")
                m = self.nc.lookup_mloc(h)
                m.allocated = True
                m.addr = self.base + st
                self.live[name] = (st, size)
                self.peak = max(self.peak, sum(s for _, s in self.live.values()))
                return h
        raise MemoryError(f"sbuf arena OOM for {name} ({size}B/part): free={self.freel}, live={ {k: v[1] for k, v in self.live.items()} }")

    def free(self, *names):
        for n in names:
            st, size = self.live.pop(n)
            self.freel.append([st, size])
        self.freel.sort()
        out = []
        for st, sz in self.freel:
            if sz == 0:
                continue
            if out and out[-1][0] + out[-1][1] == st:
                out[-1][1] += sz
            else:
                out.append([st, sz])
        self.freel = out


SEM_ALIAS = {
    "d5x0": "dx0", "d5x1": "dx1", "d5x2": "dx2",
    "d6a0": "dx0", "d6a1": "dx1", "d6b0": "dx2", "d6b1": "dx3",
    "d6c0": "dXc", "d6c1": "d5v",
    "d6xT0": "dt20", "d6xT1": "dt21",
    "dW1": "dWg", "dW2b": "dWs", "dW4": "dwd1",
    "dsw0": "dgu0", "dsw1": "dgu1", "dsw2": "dwd0",
    "g3io": "gio", "v3om": "v3ic", "v0idf": "v0c", "v0b1": "v0c", "v0lg": "v0rwT",
    "t0b1": "t0rw", "t0lg": "t0rw", "t0lt": "t0rw", "s3ic": "s0c", "v2i": "v0rwT",
}


class Sems:
    def __init__(self, nc):
        self.nc = nc
        self.h = {}
        self.v = {}

    def get(self, name):
        name = SEM_ALIAS.get(name, name)
        if name not in self.h:
            self.h[name] = self.nc.alloc_semaphore(name)
            self.v[name] = 0
        return self.h[name]

    def val(self, name):
        name = SEM_ALIAS.get(name, name)
        self.get(name)
        return self.v[name]

    def tag(self, instr, name, n=1):
        name = SEM_ALIAS.get(name, name)
        instr.then_inc(self.get(name), n)
        self.v[name] += n
        return self.v[name]


def build(debug=(), upto=99):
    nc = bass.Bass("TRN2", target_bir_lowering=False)
    nc.detect_race_conditions = False
    sb = None
    sm = Sems(nc)

    def W(eng, name, target=None):
        eng.wait_ge(sm.get(name), sm.val(name) if target is None else target)

    ev = {}

    def tagev(instr, sem_name, key, n=16):
        v = sm.tag(instr, sem_name, n)
        ev[key] = (sem_name, v)

    def Wev(eng, key):
        nm, v = ev[key]
        eng.wait_ge(sm.get(nm), v)

    dx = nc.dram_tensor("x", [N, D], F32, kind="ExternalInput").ap()
    dwq = nc.dram_tensor("wq", [D, D], F32, kind="ExternalInput").ap()
    dwk = nc.dram_tensor("wk", [D, D], F32, kind="ExternalInput").ap()
    dwv = nc.dram_tensor("wv", [D, D], F32, kind="ExternalInput").ap()
    dwo = nc.dram_tensor("wo", [D, D], F32, kind="ExternalInput").ap()
    domT = nc.dram_tensor("omT", [KH, M], F32, kind="ExternalInput").ap()
    dln1g = nc.dram_tensor("ln1g", [D], F32, kind="ExternalInput").ap()
    dln1b = nc.dram_tensor("ln1b", [D], F32, kind="ExternalInput").ap()
    dw1 = nc.dram_tensor("w1", [D, FF], F32, kind="ExternalInput").ap()
    db1 = nc.dram_tensor("b1", [FF], F32, kind="ExternalInput").ap()
    dw2 = nc.dram_tensor("w2", [FF, D], F32, kind="ExternalInput").ap()
    db2 = nc.dram_tensor("b2", [D], F32, kind="ExternalInput").ap()
    dln2g = nc.dram_tensor("ln2g", [D], F32, kind="ExternalInput").ap()
    dln2b = nc.dram_tensor("ln2b", [D], F32, kind="ExternalInput").ap()
    dWr = nc.dram_tensor("Wr", [D, E], F32, kind="ExternalInput").ap()
    dWg = nc.dram_tensor("Wg", [E, D, FEXP], F32, kind="ExternalInput").ap()
    dWu = nc.dram_tensor("Wu", [E, D, FEXP], F32, kind="ExternalInput").ap()
    dWd = nc.dram_tensor("Wd", [E, FEXP, D], F32, kind="ExternalInput").ap()
    dSg = nc.dram_tensor("Sg", [D, FS], F32, kind="ExternalInput").ap()
    dSu = nc.dram_tensor("Su", [D, FS], F32, kind="ExternalInput").ap()
    dSd = nc.dram_tensor("Sd", [FS, D], F32, kind="ExternalInput").ap()
    deb = nc.dram_tensor("eb", [E], F32, kind="ExternalInput").ap()

    dout = nc.dram_tensor("out", [N, D], F32, kind="ExternalOutput").ap()
    spill_kind = lambda nm: "ExternalOutput" if nm in debug else "Internal"
    dgspill = nc.dram_tensor("gspill", [N, D], F32, kind=spill_kind("gspill")).ap()
    dx1spill = nc.dram_tensor("x1spill", [N, D], F32, kind=spill_kind("x1spill")).ap()
    dx1T = nc.dram_tensor("x1T", [NF, 128, N], BF16, kind=spill_kind("x1T")).ap()

    dbg_h = {}

    def dump(s, name, src_ap, shape, dtype=F32):
        if name in debug:
            h = nc.dram_tensor("dbg_" + name, list(shape), dtype, kind="ExternalOutput").ap()
            dbg_h[name] = h
            sm.tag(s.dma_start(h, src_ap), "dO", 16)

    psA = nc.alloc_psum_tensor("psA", [128, 2048], F32)
    psB = nc.alloc_psum_tensor("psB", [128, 1024], F32)
    psT = nc.alloc_psum_tensor("psT", [128, 8, 128], BF16)
    psU = nc.alloc_psum_tensor("psU", [128, 8, 128], BF16)

    def psum_alias(name, bank):
        hnd = nc.virtual_psum_tensor(name, [128, 512], F32)
        nc._tensor(hnd.name, [128, 512], F32, type="PSUM")
        m = nc.lookup_mloc(hnd)
        m.allocated = True
        m.bank = bank
        return hnd
    psD = psum_alias("psD", 6)
    psE = psum_alias("psE", 7)

    ident = nc.alloc_sbuf_tensor("ident", [128, 128], BF16)
    ones1 = nc.alloc_sbuf_tensor("ones1", [1, 128], BF16)
    onesf = nc.alloc_sbuf_tensor("onesf", [1, 128], F32)
    iop = nc.alloc_sbuf_tensor("iop", [128, 128], F32)
    iof = nc.alloc_sbuf_tensor("iof", [128, 128], F32)
    io32 = nc.alloc_sbuf_tensor("io32", [32, 1], F32)
    r0s = nc.alloc_sbuf_tensor("r0s", [128, TC], F32)
    sqv = nc.alloc_sbuf_tensor("sqv", [128, 1024], F32)
    epsv0 = nc.alloc_sbuf_tensor("epsv0", [128, 1], F32)
    epsv1 = nc.alloc_sbuf_tensor("epsv1", [128, 1], F32)
    bns = nc.alloc_sbuf_tensor("bns", [128, 2, 6], F32)
    mvs = nc.alloc_sbuf_tensor("mvs", [128, TC, 2], F32)
    mvx = nc.alloc_sbuf_tensor("mvx", [128, 2], F32)
    b1s = nc.alloc_sbuf_tensor("b1s", [128, FF // 128], F32)
    b1stg = nc.alloc_sbuf_tensor("b1stg", [FF // 128, 128], F32)
    identf = nc.alloc_sbuf_tensor("identf", [128, 128], F32)
    ebs = nc.alloc_sbuf_tensor("ebs", [E, 1], F32)
    cbf = nc.alloc_sbuf_tensor("cbf", [128, TC * H], F32)
    cb32 = nc.alloc_sbuf_tensor("cb32", [128, TC, 2 * H], BF16)
    rv = nc.alloc_sbuf_tensor("rv", [128, 16], F32)
    rtm = nc.alloc_sbuf_tensor("rtm", [128, 4, 16], F32)
    rtb = nc.alloc_sbuf_tensor("rtb", [128, 2, 128], F32)
    rva = nc.alloc_sbuf_tensor("rva", [128, 8], F32)
    rvb = nc.alloc_sbuf_tensor("rvb", [128, 8], F32)

    sb = SB(nc, 201664)
    ind8 = sb.alloc("ind8", [8, 1024], BF16)
    rwT = sb.alloc("rwT", [8, N], BF16)
    xnT = sb.alloc("xnT", [128, NF, N], BF16)
    xnloT = sb.alloc("xnloT", [128, NF, N], BF16)
    xt = sb.alloc("xt", [128, 4, D], F32)
    xn32 = sb.alloc("xn32", [128, 2, D], F32)
    xnb = sb.alloc("xnb", [128, 2, D], BF16)
    xnlo = sb.alloc("xnlo", [128, 2, D], BF16)
    wrh = sb.alloc("wrh", [128, NF, E], BF16)
    wrl = sb.alloc("wrl", [128, NF, E], BF16)
    wrf = sb.alloc("wrf", [128, NF, E], F32)
    lgT = sb.alloc("lgT", [8, N], F32)
    rwt = sb.alloc("rwt", [128, TC, 8], F32)
    rwb16 = sb.alloc("rwb16", [128, TC, 8], BF16)

    with nc.Block("p0", no_gpsimd_drain=True) as blk:
        @blk.sync
        def _(s):
            sm.tag(s.dma_start(wrf[:], dWr.rearrange("(c p) e -> p c e", p=128)), "dXc", 16)
            sm.tag(s.dma_start(ebs[:], deb[:, None]), "dXc", 16)
            tagev(s.dma_start(b1stg[:], db1.rearrange("(c f) -> c f", f=128)), "dXc", ("xc",))
            for tc in range(TC):
                if tc >= 4:
                    W(s, "v0xn", tc - 3)
                tagev(s.dma_start(xt[:, tc % 4, :], dx[128 * tc:128 * (tc + 1), :]), "dx%d" % (tc % 4), ("x0", tc))

        @blk.gpsimd
        def _(g):
            g.iota(iop[:], pattern=[[0, 128]], channel_multiplier=1, allow_small_or_imprecise_dtypes=True)
            g.iota(iof[:], pattern=[[1, 128]], channel_multiplier=0, allow_small_or_imprecise_dtypes=True)
            g.iota(io32[:], pattern=[[0, 1]], channel_multiplier=1, allow_small_or_imprecise_dtypes=True)
            sm.tag(g.iota(ind8[:], pattern=[[1, E], [0, 128]], channel_multiplier=0, allow_small_or_imprecise_dtypes=True), "gio")

        @blk.vector
        def _(v):
            W(v, "gio")
            v.tensor_tensor(ident[:], iop[:], iof[:], op=OP.is_equal)
            v.memset(ones1[:], 1.0)
            v.memset(onesf[:], 1.0)
            v.tensor_scalar(rv[0:32, 0:1], io32[:], -1.0, 0.5, op0=OP.mult, op1=OP.add)
            v.tensor_scalar(rv[0:32, 1:2], io32[:], -1.0, None, op0=OP.mult)
            v.memset(rv[0:32, 9:10], -1.0)
            v.memset(epsv0[:], EPS0)
            v.memset(epsv1[:], EPS1)
            sm.tag(v.memset(rv[0:32, 10:11], -0.5), "v0c")

        @blk.scalar
        def _(s):
            W(s, "v0c")
            sm.tag(s.activation(ind8[:], ind8[:], AF.Abs, bias=rv[0:8, 1:2], scale=1.0), "sch")
            W(s, "sch")
            sm.tag(s.activation(ind8[:], ind8[:], AF.Sign, bias=rv[0:8, 10:11], scale=1.0), "s0c")

        @blk.vector
        def _(v):
            W(v, "s0c")
            v.tensor_scalar(ind8[:], ind8[:], -0.5, 0.5, op0=OP.mult, op1=OP.add)
            Wev(v, ("xc",))
            v.tensor_copy(wrh[:].rearrange("p c e -> p (c e)"), wrf[:].rearrange("p c e -> p (c e)"))
            sm.tag(v.tensor_tensor(wrl[:].rearrange("p c e -> p (c e)"), wrf[:].rearrange("p c e -> p (c e)"),
                                   wrh[:].rearrange("p c e -> p (c e)"), op=OP.subtract), "v0c")

        @blk.scalar
        def _(s):
            for tc in range(TC):
                W(s, "v0var", tc + 1)
                sm.tag(s.activation(rv[:, 4:5], mvs[:, tc, 1:2], AF.Ln, bias=epsv0[:]), "sch")
                W(s, "sch")
                sm.tag(s.activation(r0s[:, tc:tc + 1], rv[:, 4:5], AF.Exp, scale=-0.5), "s0r")
                W(s, "v0x32", tc + 1)
                if tc >= 2:
                    W(s, "t0h", tc - 1)
                sm.tag(s.copy(xnb[:, tc % 2, :], xn32[:, tc % 2, :]), "s0xb")
                W(s, "t0h", tc + 1)
                sm.tag(s.copy(xnT[:, :, 128 * tc:128 * (tc + 1)], psT[:]), "v0evh")
                W(s, "t0l", tc + 1)
                sm.tag(s.copy(xnloT[:, :, 128 * tc:128 * (tc + 1)], psU[:]), "v0evl")

        @blk.vector
        def _(v):
            for tc in range(TC):
                Wev(v, ("x0", tc))
                v.bn_stats(bns[:, 0, :], xt[:, tc % 4, 0:512])
                sm.tag(v.bn_stats(bns[:, 1, :], xt[:, tc % 4, 512:1024]), "vch")
                W(v, "vch")
                sm.tag(v.bn_aggr(mvs[:, tc, :], bns[:].rearrange("p a b -> p (a b)")), "v0var")
                W(v, "s0r", tc + 1)
                if tc >= 2:
                    W(v, "s0xb", tc - 1)
                sm.tag(v.tensor_scalar(xn32[:, tc % 2, :], xt[:, tc % 4, :], mvs[:, tc, 0:1],
                                       r0s[:, tc:tc + 1], op0=OP.subtract, op1=OP.mult), "v0x32")
                W(v, "s0xb", tc + 1)
                if tc >= 2:
                    W(v, "t0l", tc - 1)
                sm.tag(v.tensor_tensor(xnlo[:, tc % 2, :], xn32[:, tc % 2, :], xnb[:, tc % 2, :],
                                       op=OP.subtract), "v0xn")

        @blk.tensor
        def _(t):
            W(t, "v0c")
            for tc in range(TC):
                W(t, "v0xn", tc + 1)
                if tc >= 1:
                    W(t, "v0evh", tc)
                for c in range(NF):
                    ins = t.transpose(psT[:, c, :], xnb[:, tc % 2, 128 * c:128 * (c + 1)], ident[:])
                sm.tag(ins, "t0h")
                if tc >= 1:
                    W(t, "v0evl", tc)
                for c in range(NF):
                    ins = t.transpose(psU[:, c, :], xnlo[:, tc % 2, 128 * c:128 * (c + 1)], ident[:])
                sm.tag(ins, "t0l")

    with nc.Block("p0r", no_gpsimd_drain=True) as blk:
        @blk.vector
        def _(v):
            sm.tag(v.tensor_copy(identf[:], ident[:]), "v0idf")

        @blk.tensor
        def _(t):
            W(t, "v0idf")
            Wev(t, ("xc",))
            sm.tag(t.transpose(psB[:, 0:32], b1stg[:], identf[0:32, 0:32]), "t0b1")

        @blk.vector
        def _(v):
            W(v, "t0b1")
            sm.tag(v.tensor_copy(b1s[:], psB[:, 0:32]), "v0b1")

        @blk.tensor
        def _(t):
            for pc in range(4):
                ops = [(c, lw, rx) for c in range(NF) for (lw, rx) in ((wrh, xnT), (wrl, xnT), (wrh, xnloT))]
                for i, (c, lw, rx) in enumerate(ops):
                    ins = t.matmul(psA[0:8, 512 * pc:512 * (pc + 1)], lw[:, c, :],
                                   rx[:, c, 512 * pc:512 * (pc + 1)], start=(i == 0), stop=(i == len(ops) - 1))
            sm.tag(ins, "t0lg")

        @blk.vector
        def _(v):
            W(v, "t0lg")
            sm.tag(v.tensor_scalar(lgT[:], psA[0:8, :], ebs[:], None, op0=OP.add), "v0lg")

        @blk.tensor
        def _(t):
            W(t, "v0lg")
            W(t, "v0b1")
            for tc in range(TC):
                ins = t.transpose(psB[:, 8 * tc:8 * (tc + 1)], lgT[:, 128 * tc:128 * (tc + 1)], identf[0:8, 0:8])
            sm.tag(ins, "t0lt")

        @blk.vector
        def _(v):
            W(v, "t0lt")
            lg3 = psB[:, 0:128].rearrange("p (g e) -> p g e", e=8)
            def bc(col):
                return rtm[:, col, :].rearrange("p (g one) -> p g one", one=1).broadcast_to([128, TC, 8])
            sm.tag(v.tensor_reduce(rtm[:, 0, :], lg3, axis=AX.X, op=OP.max), "vch")
            W(v, "vch")
            sm.tag(v.tensor_tensor(rtb[:, 0, :].rearrange("p (g e) -> p g e", e=8), lg3, bc(0),
                                   op=OP.subtract), "v0r1")
            W(v, "s0m1", 1)
            v.tensor_scalar(rtb[:, 1, :], rtb[:, 1, :], -1.0, 0.0, op0=OP.mult, op1=OP.max)
            v.tensor_scalar(rtb[:, 1, :], rtb[:, 1, :], 1.0, 1e30, op0=OP.subtract, op1=OP.mult)
            v.tensor_tensor(rtb[:, 1, :], psB[:, 0:128], rtb[:, 1, :], op=OP.add)
            sm.tag(v.tensor_reduce(rtm[:, 1, :], rtb[:, 1, :].rearrange("p (g e) -> p g e", e=8),
                                   axis=AX.X, op=OP.max), "vch")
            W(v, "vch")
            sm.tag(v.tensor_tensor(rtb[:, 1, :].rearrange("p (g e) -> p g e", e=8), lg3, bc(1),
                                   op=OP.subtract), "v0r2")
            sm.tag(v.tensor_tensor(rtm[:, 2, :], rtm[:, 1, :], rtm[:, 0, :], op=OP.subtract), "v0r3")
            W(v, "s0e", 1)
            sm.tag(v.tensor_scalar(rtm[:, 3, :], rtm[:, 2, :], 1.0, None, op0=OP.add), "vch")
            W(v, "vch")
            sm.tag(v.reciprocal(rtm[:, 3, :], rtm[:, 3, :]), "vch")
            W(v, "vch")
            v.tensor_scalar(rtb[:, 1, :], rtb[:, 1, :], -1.0, 0.0, op0=OP.mult, op1=OP.max)
            v.tensor_scalar(rtb[:, 1, :], rtb[:, 1, :], -1.0, 1.0, op0=OP.mult, op1=OP.add)
            v.tensor_tensor(rtb[:, 0, :], rtb[:, 0, :], rtb[:, 1, :], op=OP.mult)
            v.tensor_tensor(rwt[:].rearrange("p t e -> p t e"), rtb[:, 0, :].rearrange("p (g e) -> p g e", e=8),
                            bc(3), op=OP.mult)
            sm.tag(v.tensor_copy(rwb16[:].rearrange("p t e -> p (t e)"), rwt[:].rearrange("p t e -> p (t e)")), "v0rw")

        @blk.scalar
        def _(s):
            W(s, "v0r1", 1)
            sm.tag(s.activation(rtb[:, 1, :], rtb[:, 0, :], AF.Sign), "s0m1")
            W(s, "v0r2", 1)
            s.activation(rtb[:, 1, :], rtb[:, 1, :], AF.Sign)
            s.activation(rtb[:, 0, :], rtb[:, 0, :], AF.Exp)
            W(s, "v0r3", 1)
            sm.tag(s.activation(rtm[:, 2, :], rtm[:, 2, :], AF.Exp), "s0e")

        b_rwT = sm.val("v0rwT")
        b_t0rw = sm.val("t0rw")

        @blk.tensor
        def _(t):
            for tc in range(TC):
                W(t, "v0rw", 1)
                if tc >= 2:
                    W(t, "v0rwT", b_rwT + tc - 1)
                ps = psT if tc % 2 == 0 else psU
                sm.tag(t.transpose(ps[0:8, 0, :], rwb16[:, tc, :], ident[:]), "t0rw")

        @blk.vector
        def _(v):
            for tc in range(TC):
                W(v, "t0rw", b_t0rw + tc + 1)
                ps = psT if tc % 2 == 0 else psU
                sm.tag(v.tensor_copy(rwT[:, 128 * tc:128 * (tc + 1)], ps[0:8, 0, :]), "v0rwT")

        @blk.sync
        def _(s):
            W(s, "v0rwT", b_rwT + TC)
            dump(s, "rw", rwt[:].rearrange("p t e -> p (t e)"), [128, TC * 8])
            dump(s, "lgT", lgT[:], [8, N])
            dump(s, "m0", mvs[:].rearrange("p t x -> p (t x)"), [128, TC * 2])
            dump(s, "r0", r0s[:], [128, TC])
            dump(s, "xnTd", xnT[:].rearrange("p c n -> p (c n)"), [128, NF * N], BF16)

    sb.free("xt", "xn32", "xnb", "xnlo", "wrh", "wrl", "wrf", "lgT", "xnloT", "rwt", "rwb16")
    if upto <= 0:
        return fin(nc, sm, dbg_h)

    moe = sb.alloc("moe", [128, TC, D], F32)
    hpr = sb.alloc("hpr", [128, 2, 4, N], BF16)
    wgb = sb.alloc("wgb", [128, 2, NF, FEXP], BF16)
    wub = sb.alloc("wub", [128, 2, NF, FEXP], BF16)
    wdb = sb.alloc("wdb", [128, 2, 4, D], BF16)
    rwbb = sb.alloc("rwbb", [128, 1, N], BF16)
    sgl = sb.alloc("sgl", [128, 2, 512], BF16)
    up1 = sb.alloc("up1", [128, 2, 512], BF16)
    NQ = 16

    def qgate(gi):
        sl = gi % 3
        return psA[:, 1024 * sl:1024 * sl + 512] if sl < 2 else psD[:]

    def qup(gi):
        sl = gi % 3
        return psA[:, 1024 * sl + 512:1024 * sl + 1024] if sl < 2 else psE[:]

    def quarters_T(t, base, wg_l, wu_l, has_up=True, post=None, rwv=None):
        first2 = True
        for fk in range(4):
            for q in range(4):
                i = 4 * fk + q
                gi = base + i
                if gi % 3 == 2 and first2 and rwv is not None:
                    W(t, "v1rwb", rwv)
                    first2 = False
                if gi >= 3:
                    W(t, "s1sg", gi - 2)
                    if has_up:
                        W(t, "v1u", gi - 2)
                    else:
                        W(t, "v1h", gi - 2)
                for dc in range(NF):
                    insg = t.matmul(qgate(gi), wg_l(dc, fk),
                                    xnT[:, dc, 512 * q:512 * (q + 1)], start=(dc == 0), stop=(dc == NF - 1))
                    insu = t.matmul(qup(gi), wu_l(dc, fk),
                                    xnT[:, dc, 512 * q:512 * (q + 1)], start=(dc == 0), stop=(dc == NF - 1))
                sm.tag(insg, "t1g")
                sm.tag(insu, "t1u")
                if post is not None:
                    post(i)

    with nc.Block("p1", no_gpsimd_drain=True) as blk:
        @blk.gpsimd
        def _(g):
            W(g, "dO")
            for e in range(E):
                if e >= 2:
                    W(g, "t1u", NQ * (e - 1))
                if e == 0:
                    srcg = dWg[0].rearrange("(c p) f -> p c f", p=128)
                    srcu = dWu[0].rearrange("(c p) f -> p c f", p=128)
                    tagev(g.dma_start(wgb[:, 0, 0:4], srcg[:, 0:4]), "dgu0", ("g00",))
                    tagev(g.dma_start(wgb[:, 0, 4:8], srcg[:, 4:8]), "dgu1", ("g01",))
                    tagev(g.dma_start(wub[:, 0, 0:4], srcu[:, 0:4]), "dwd0", ("g02",))
                    tagev(g.dma_start(wub[:, 0, 4:8], srcu[:, 4:8]), "dwd1", ("g03",))
                else:
                    sm.tag(g.dma_start(wgb[:, e % 2], dWg[e].rearrange("(c p) f -> p c f", p=128)), "dgu%d" % (e % 2), 16)
                    tagev(g.dma_start(wub[:, e % 2], dWu[e].rearrange("(c p) f -> p c f", p=128)), "dgu%d" % (e % 2), ("gu", e))
                if e >= 2:
                    W(g, "t1wd", TC * ((e - 2) // 2 + 1))
                tagev(g.dma_start(wdb[:, e % 2], dWd[e].rearrange("(c p) f -> p c f", p=128)), "dwd%d" % (e % 2), ("wd", e))

        @blk.tensor
        def _(t):
            def dtc(elead, tc):
                if tc == 0:
                    Wev(t, ("wd", elead))
                    Wev(t, ("wd", elead + 1))
                    W(t, "v1h", NQ * (elead + 2))
                W(t, "v1acc", TC * (elead // 2) + tc)
                for j, ee in enumerate((elead, elead + 1)):
                    for fk in range(4):
                        for half in range(2):
                            ins = t.matmul(psB[:, 512 * half:512 * (half + 1)],
                                           hpr[:, ee % 2, fk, 128 * tc:128 * (tc + 1)],
                                           wdb[:, ee % 2, fk, 512 * half:512 * (half + 1)],
                                           start=(j == 0 and fk == 0), stop=(j == 1 and fk == 3))
                sm.tag(ins, "t1wd")

            for e in range(E):
                if e == 0:
                    for k2 in (("g00",), ("g01",), ("g02",), ("g03",)):
                        Wev(t, k2)
                else:
                    Wev(t, ("gu", e))
                if e >= 1:
                    last2 = max(g2 for g2 in range(NQ * e) if g2 % 3 == 2)
                    W(t, "s1sg", last2 + 1)
                    W(t, "v1u", last2 + 1)
                for hf in range(2):
                    W(t, "v1rwb", 2 * e + hf)
                    t.matmul(psD[:], ind8[:, 128 * e:128 * (e + 1)],
                             rwT[:, 1024 * hf:1024 * hf + 512], start=True, stop=True)
                    ins = t.matmul(psE[:], ind8[:, 128 * e:128 * (e + 1)],
                                   rwT[:, 1024 * hf + 512:1024 * (hf + 1)], start=True, stop=True)
                    sm.tag(ins, "t1rwb")
                quarters_T(t, NQ * e,
                           lambda dc, fk, e=e: wgb[:, e % 2, dc, 128 * fk:128 * (fk + 1)],
                           lambda dc, fk, e=e: wub[:, e % 2, dc, 128 * fk:128 * (fk + 1)],
                           rwv=2 * e + 2)
                if e % 2 == 1:
                    for tc in range(TC):
                        dtc(e - 1, tc)

        @blk.scalar
        def _(s):
            for e in range(E):
                for i in range(NQ):
                    gi = NQ * e + i
                    W(s, "t1g", gi + 1)
                    if gi >= 2:
                        W(s, "v1h", gi - 1)
                    sm.tag(s.activation(sgl[:, gi % 2, :], qgate(gi), AF.Silu), "s1sg")

        @blk.vector
        def _(v):
            def acc(elead, tc):
                W(v, "t1wd", TC * (elead // 2) + tc + 1)
                if elead // 2 == 0:
                    sm.tag(v.tensor_copy(moe[:, tc, :], psB[:]), "v1acc")
                else:
                    sm.tag(v.tensor_tensor(moe[:, tc, :], moe[:, tc, :], psB[:], op=OP.add), "v1acc")

            for e in range(E):
                for hf in range(2):
                    W(v, "t1rwb", 2 * e + hf + 1)
                    v.tensor_copy(rwbb[:, 0, 1024 * hf:1024 * hf + 512], psD[:])
                    sm.tag(v.tensor_copy(rwbb[:, 0, 1024 * hf + 512:1024 * (hf + 1)], psE[:]), "v1rwb")
                for i in range(NQ):
                    gi = NQ * e + i
                    fk, q = i // 4, i % 4
                    W(v, "t1u", gi + 1)
                    sm.tag(v.tensor_tensor(up1[:, gi % 2, :], qup(gi),
                                           rwbb[:, 0, 512 * q:512 * (q + 1)], op=OP.mult), "v1u")
                    W(v, "s1sg", gi + 1)
                    sm.tag(v.tensor_tensor(hpr[:, e % 2, fk, 512 * q:512 * (q + 1)],
                                           sgl[:, gi % 2, :], up1[:, gi % 2, :], op=OP.mult), "v1h")
                if e % 2 == 1:
                    for tc in range(TC):
                        acc(e - 1, tc)

    sb.free("wgb", "wub", "wdb", "hpr", "rwbb")
    sgw = sb.alloc("sgw", [128, 2, NF, 512], BF16)
    suw = sb.alloc("suw", [128, 2, NF, 512], BF16)
    sdw = sb.alloc("sdw", [128, 2, 4, D], BF16)
    hsh = sb.alloc("hsh", [128, 2, 4, N], BF16)
    gsp0 = sb.alloc("gsp0", [128, D], F32)
    gsp1 = sb.alloc("gsp1", [128, D], F32)

    with nc.Block("p1s", no_gpsimd_drain=True) as blk:
        @blk.gpsimd
        def _(g):
            for s2 in range(2):
                tagev(g.dma_start(sgw[:, s2], dSg[:, 512 * s2:512 * (s2 + 1)].rearrange("(c p) f -> p c f", p=128)), "dsw0", ("wsg", s2))
                tagev(g.dma_start(suw[:, s2], dSu[:, 512 * s2:512 * (s2 + 1)].rearrange("(c p) f -> p c f", p=128)), "dsw1", ("wsu", s2))
                tagev(g.dma_start(sdw[:, s2], dSd[512 * s2:512 * (s2 + 1), :].rearrange("(c p) f -> p c f", p=128)), "dsw2", ("wsd", s2))

        @blk.tensor
        def _(t):
            def sdtc(s2, tc):
                if tc == 0:
                    Wev(t, ("wsd", s2))
                    W(t, "v1h", NQ * E + NQ * (s2 + 1))
                W(t, "v1acc", TC * (4 + s2) + tc)
                for fk in range(4):
                    for half in range(2):
                        ins = t.matmul(psB[:, 512 * half:512 * (half + 1)],
                                       hsh[:, s2, fk, 128 * tc:128 * (tc + 1)],
                                       sdw[:, s2, fk, 512 * half:512 * (half + 1)],
                                       start=(fk == 0), stop=(fk == 3))
                sm.tag(ins, "t1sd")

            for s2 in range(2):
                Wev(t, ("wsg", s2))
                Wev(t, ("wsu", s2))
                post = (lambda i: sdtc(0, i)) if s2 == 1 else None
                quarters_T(t, NQ * E + NQ * s2,
                           lambda dc, fk, s2=s2: sgw[:, s2, dc, 128 * fk:128 * (fk + 1)],
                           lambda dc, fk, s2=s2: suw[:, s2, dc, 128 * fk:128 * (fk + 1)], has_up=False, post=post,
                           rwv=2 * E if s2 == 0 else None)
            for tc in range(TC):
                sdtc(1, tc)

        @blk.scalar
        def _(s):
            for s2 in range(2):
                for i in range(NQ):
                    gi = NQ * E + NQ * s2 + i
                    W(s, "t1g", gi + 1)
                    if gi >= 2:
                        W(s, "v1h", gi - 1)
                    sm.tag(s.activation(sgl[:, gi % 2, :], qgate(gi), AF.Silu), "s1sg")
            for tc in range(TC):
                W(s, "v1acc", TC * 5 + tc + 1)
                if tc >= 2:
                    s.wait_ge(sm.get("d1g%d" % ((tc - 2) % 2)), 16 * ((tc - 2) // 2 + 1))
                sm.tag(s.activation((gsp0 if tc % 2 == 0 else gsp1)[:], moe[:, tc, :], AF.Silu), "s1g")

        @blk.vector
        def _(v):
            for s2 in range(2):
                for i in range(NQ):
                    gi = NQ * E + NQ * s2 + i
                    fk, q = i // 4, i % 4
                    W(v, "t1u", gi + 1)
                    W(v, "s1sg", gi + 1)
                    sm.tag(v.tensor_tensor(hsh[:, s2, fk, 512 * q:512 * (q + 1)],
                                           sgl[:, gi % 2, :], qup(gi), op=OP.mult), "v1h")
                    if s2 == 1:
                        W(v, "t1sd", i + 1)
                        sm.tag(v.tensor_tensor(moe[:, i, :], moe[:, i, :], psB[:], op=OP.add), "v1acc")
            for tc in range(TC):
                W(v, "t1sd", TC + tc + 1)
                sm.tag(v.tensor_tensor(moe[:, tc, :], moe[:, tc, :], psB[:], op=OP.add), "v1acc")

        @blk.sync
        def _(s):
            for tc in range(TC):
                W(s, "s1g", tc + 1)
                tagev(s.dma_start(dgspill[128 * tc:128 * (tc + 1), :], (gsp0 if tc % 2 == 0 else gsp1)[:]), "d1g%d" % (tc % 2), ("g1", tc))
            Wev(s, ("g1", TC - 2))
            Wev(s, ("g1", TC - 1))
            dump(s, "moe", moe[:].rearrange("p t d -> p (t d)"), [128, TC * D])

    sb.free("sgw", "suw", "sdw", "hsh", "gsp0", "gsp1", "moe", "ind8", "rwT")
    if upto <= 1:
        return fin(nc, sm, dbg_h)

    sqq = sb.alloc("sqq", [128, TC, H], F32)
    sqk = sb.alloc("sqk", [128, TC, H], F32)
    nstq = sb.alloc("nstq", [128, TC, H], F32)
    nstk = sb.alloc("nstk", [128, TC, H], F32)
    qT = sb.alloc("qT", [128, NF, N], BF16)
    kT = sb.alloc("kT", [128, NF, N], BF16)
    vaug = sb.alloc("vaug", [128, TC, H * 65], BF16)
    wsw = sb.alloc("wsw", [128, 2, NF, D], BF16)
    qkb = sb.alloc("qkb", [128, 2, D], BF16)
    sqscr = sb.alloc("sqscr", [128, 2, D], F32)

    sweeps = (("q", dwq, qT, sqq), ("k", dwk, kT, sqk), ("v", dwv, None, None))

    with nc.Block("p2", no_gpsimd_drain=True) as blk:
        @blk.gpsimd
        def _(g):
            W(g, "dO")
            Wev(g, ("g1", TC - 2))
            Wev(g, ("g1", TC - 1))
            for si, (nm, dw, tT, sq) in enumerate(sweeps):
                if si >= 2:
                    W(g, "t2mm", TC * (si - 1))
                tagev(g.dma_start(wsw[:, si % 2], dw.rearrange("(c p) f -> p c f", p=128)), "dsw%d" % si, ("sw", si))

        @blk.vector
        def _(v):
            sm.tag(v.memset(vaug[:].rearrange("p t w -> p (t w)"), 1.0), "v2i")

        @blk.tensor
        def _(t):
            def tr2(g):
                W(t, "v2qk", g + 1)
                if g >= 2:
                    W(t, "s2ev", g - 1)
                pT = psT if g % 2 == 0 else psU
                for c in range(NF):
                    ins = t.transpose(pT[:, c, 0:128], qkb[:, g % 2, 128 * c:128 * (c + 1)], ident[:])
                sm.tag(ins, "t2tr")

            W(t, "v2i")
            for si, (nm, dw, tT, sq) in enumerate(sweeps):
                Wev(t, ("sw", si))
                for tc in range(TC):
                    gi = TC * si + tc
                    if gi >= 2:
                        if gi - 2 < 2 * TC:
                            W(t, "v2qk", gi - 1)
                            W(t, "s2sq", gi - 1)
                        else:
                            W(t, "v2ev", gi - 1 - 2 * TC)
                    for half in range(2):
                        for dc in range(NF):
                            ins = t.matmul(psA[:, 1024 * (gi % 2) + 512 * half:1024 * (gi % 2) + 512 * (half + 1)],
                                           xnT[:, dc, 128 * tc:128 * (tc + 1)],
                                           wsw[:, si % 2, dc, 512 * half:512 * (half + 1)], start=(dc == 0), stop=(dc == NF - 1))
                    sm.tag(ins, "t2mm")
                    if 1 <= gi <= 2 * TC:
                        tr2(gi - 1)

        @blk.scalar
        def _(s):
            def ev2(g):
                W(s, "t2tr", g + 1)
                tTp = qT if g < TC else kT
                tcp = g % TC
                pT = psT if g % 2 == 0 else psU
                sm.tag(s.copy(tTp[:, 0:NF, 128 * tcp:128 * (tcp + 1)], pT[:, 0:NF, 0:128]), "s2ev")

            for si in range(2):
                for tc in range(TC):
                    gi = TC * si + tc
                    W(s, "t2mm", gi + 1)
                    if gi >= 2:
                        W(s, "v2qk", gi - 1)
                    sm.tag(s.activation(sqscr[:, gi % 2, :], psA[:, 1024 * (gi % 2):1024 * (gi % 2) + 1024], AF.Square), "s2sq")
                    if gi >= 1:
                        ev2(gi - 1)
            ev2(2 * TC - 1)

        @blk.vector
        def _(v):
            for si, (nm, dw, tT, sq) in enumerate(sweeps):
                for tc in range(TC):
                    gi = TC * si + tc
                    W(v, "t2mm", gi + 1)
                    src = psA[:, 1024 * (gi % 2):1024 * (gi % 2) + 1024]
                    if si < 2:
                        W(v, "s2sq", gi + 1)
                        sm.tag(v.tensor_reduce(sq[:, tc, :], sqscr[:, gi % 2, :].rearrange("p (h w) -> p h w", w=KH),
                                               axis=AX.X, op=OP.add), "v2sq")
                        if gi >= 2:
                            W(v, "t2tr", gi - 1)
                        sm.tag(v.tensor_copy(qkb[:, gi % 2, :], src), "v2qk")
                    else:
                        sm.tag(v.tensor_copy(vaug[:, tc, :].rearrange("p (h w) -> p h w", w=65)[:, :, 0:64],
                                             src.rearrange("p (h w) -> p h w", w=KH)), "v2ev")

        @blk.sync
        def _(s):
            W(s, "s2ev", 2 * TC)
            W(s, "v2qk", 2 * TC)
            W(s, "v2ev", TC)
            dump(s, "qTd", qT[:].rearrange("p c n -> p (c n)"), [128, NF * N], BF16)
            dump(s, "kTd", kT[:].rearrange("p c n -> p (c n)"), [128, NF * N], BF16)
            dump(s, "vaugd", vaug[:].rearrange("p t w -> p (t w)"), [128, TC * H * 65], BF16)
            dump(s, "sqqd", sqq[:].rearrange("p t h -> p (t h)"), [128, TC * H])
            dump(s, "sqkd", sqk[:].rearrange("p t h -> p (t h)"), [128, TC * H])

    sb.free("wsw", "qkb", "sqscr", "xnT")
    if upto <= 2:
        return fin(nc, sm, dbg_h)

    omT = sb.alloc("omTs", [KH, M], BF16)
    omTD = sb.alloc("omTD", [128, M], BF16)
    ombd = sb.alloc("ombd", [128, 512], BF16)
    cqT = sb.alloc("cqT", [32, N], BF16)
    ckT = sb.alloc("ckT", [32, N], BF16)
    indq = sb.alloc("indq", [32, 2048], BF16)
    indk = sb.alloc("indk", [32, 4096], BF16)

    with nc.Block("p3a", no_gpsimd_drain=True) as blk:
        @blk.gpsimd
        def _(g):
            W(g, "dO")
            tagev(g.dma_start(omT[:], domT), "dWg", ("om",))
            g.iota(indq[:], pattern=[[2, H], [0, 128]], channel_multiplier=0, allow_small_or_imprecise_dtypes=True)
            sm.tag(g.iota(indk[:], pattern=[[2, H], [0, 256]], channel_multiplier=0, allow_small_or_imprecise_dtypes=True), "g3io")

        @blk.vector
        def _(v):
            W(v, "g3io")
            Wev(v, ("om",))
            v.tensor_scalar(omT[:], omT[:], SCALE, None, op0=OP.mult)
            v.memset(ombd[:], 0.0)
            v.tensor_copy(ombd[0:64, 0:256], omT[:])
            v.tensor_copy(omTD[0:64, :], omT[:])
            v.tensor_copy(omTD[64:128, :], omT[:])
            sm.tag(v.tensor_copy(ombd[64:128, 256:512], omT[:]), "v3om")
            v.tensor_scalar(rv[0:32, 0:1], io32[:], -1.0, 0.5, op0=OP.mult, op1=OP.add)
            sm.tag(v.memset(rv[0:32, 9:10], -1.0), "v3ic")

        @blk.scalar
        def _(s):
            W(s, "v3ic")
            sm.tag(s.activation(indq[:], indq[:], AF.Abs, bias=rv[0:32, 0:1], scale=1.0), "sch")
            W(s, "sch")
            sm.tag(s.activation(indq[:], indq[:], AF.Sign, bias=rv[0:32, 9:10], scale=1.0), "sch")
            W(s, "sch")
            sm.tag(s.activation(indk[:], indk[:], AF.Abs, bias=rv[0:32, 0:1], scale=1.0), "sch")
            W(s, "sch")
            sm.tag(s.activation(indk[:], indk[:], AF.Sign, bias=rv[0:32, 9:10], scale=1.0), "s3ic")

        @blk.vector
        def _(v):
            W(v, "s3ic")
            v.tensor_scalar(indq[:], indq[:], -0.5, 0.5, op0=OP.mult, op1=OP.add)
            sm.tag(v.tensor_scalar(indk[:], indk[:], -0.5, 0.5, op0=OP.mult, op1=OP.add), "v3ic")

        @blk.tensor
        def _(t):
            W(t, "v3om")
            for si, tT in enumerate((kT, qT)):
                for tc in range(TC):
                    for g2 in range(4):
                        gi = 64 * si + 4 * tc + g2
                        if gi >= 2:
                            W(t, "v3st", gi - 1)
                        for j in range(2):
                            fc = 2 * g2 + j
                            ins = t.matmul(psA[:, 1024 * (gi % 2) + 512 * j:1024 * (gi % 2) + 512 * (j + 1)],
                                           tT[:, fc, 128 * tc:128 * (tc + 1)], ombd[:], start=True, stop=True)
                        sm.tag(ins, "t3st")

        @blk.vector
        def _(v):
            for si, nst in enumerate((nstk, nstq)):
                for tc in range(TC):
                    for g2 in range(4):
                        gi = 64 * si + 4 * tc + g2
                        W(v, "t3st", gi + 1)
                        sm.tag(v.tensor_reduce(nst[:, tc, 4 * g2:4 * (g2 + 1)],
                                               psA[:, 1024 * (gi % 2):1024 * (gi % 2) + 1024].rearrange("p (h w) -> p h w", w=M),
                                               axis=AX.X, op=OP.max, negate=True), "v3st")

        @blk.vector
        def _(v):
            W(v, "v3st", 128)
            W(v, "v2sq", 32)
            for si, (sq, nst, cT) in enumerate(((sqk, nstk, ckT), (sqq, nstq, cqT))):
                sqf = sq[:].rearrange("p t h -> p (t h)")
                nstf = nst[:].rearrange("p t h -> p (t h)")
                v.tensor_scalar(cbf[:], sqf, SQC, LNRTM, op0=OP.mult, op1=OP.add)
                sm.tag(v.tensor_tensor(cbf[:], nstf, cbf[:], op=OP.subtract), "vch")
                W(v, "vch")
                ev3 = cb32[:].rearrange("p t (h two) -> p (t h) two", two=2)
                sm.tag(v.tensor_copy(ev3[:, :, 0:1].rearrange("p a one -> p (a one)"), cbf[:]), "vch")
                W(v, "vch")
                sm.tag(v.tensor_tensor(ev3[:, :, 1:2].rearrange("p a one -> p (a one)"), cbf[:],
                                       ev3[:, :, 0:1].rearrange("p a one -> p (a one)"), op=OP.subtract), "v3c")
                for tc in range(TC):
                    W(v, "t3c", 16 * si + tc + 1)
                    ps = psT if (16 * si + tc) % 2 == 0 else psU
                    sm.tag(v.tensor_copy(cT[:, 128 * tc:128 * (tc + 1)], ps[0:32, 0, :]), "v3cT")

        @blk.tensor
        def _(t):
            for si in range(2):
                W(t, "v3c", si + 1)
                for tc in range(TC):
                    if 16 * si + tc >= 2:
                        W(t, "v3cT", 16 * si + tc - 1)
                    ps = psT if (16 * si + tc) % 2 == 0 else psU
                    sm.tag(t.transpose(ps[0:32, 0, :], cb32[:, tc, :], ident[:]), "t3c")

        @blk.sync
        def _(s):
            W(s, "v3cT", 32)
            dump(s, "nstkd", nstk[:].rearrange("p t h -> p (t h)"), [128, TC * H])
            dump(s, "nstqd", nstq[:].rearrange("p t h -> p (t h)"), [128, TC * H])
            dump(s, "ckTd", ckT[:], [32, N], BF16)
            dump(s, "cqTd", cqT[:], [32, N], BF16)

    sb.free("sqq", "sqk", "nstq", "nstk")
    if upto <= 3:
        return fin(nc, sm, dbg_h)

    pk = sb.alloc("pk", [128, TC, 512], BF16)
    pse = sb.alloc("pse", [128, 2, 512], F32)
    kvzTs = sb.alloc("kvzTs", [65, 512], BF16)
    kvz = sb.alloc("kvz", [128, 32, 65], BF16)

    with nc.Block("p3b", no_gpsimd_drain=True) as blk:
        @blk.tensor
        def _(t):
            for fc in range(NF):
                for tc in range(TC):
                    gi = TC * fc + tc
                    if gi >= 4:
                        W(t, "s3e", gi - 3)
                    sl = gi % 4
                    t.matmul(psA[:, 512 * sl:512 * (sl + 1)], kT[:, fc, 128 * tc:128 * (tc + 1)],
                             ombd[:], start=True, stop=False)
                    sm.tag(t.matmul(psA[:, 512 * sl:512 * (sl + 1)], ckT[:, 128 * tc:128 * (tc + 1)],
                                    indk[:, 512 * fc:512 * (fc + 1)], start=False, stop=True), "t3p")
                    if tc >= 1:
                        W(t, "v3pk", TC * fc + tc)
                        for hh in range(2):
                            ins = t.matmul(psB[0:65, 512 * hh:512 * hh + 256],
                                           vaug[:, tc - 1, :].rearrange("p (h w) -> p h w", w=65)[:, 2 * fc + hh, :],
                                           pk[:, tc - 1, 256 * hh:256 * (hh + 1)],
                                           start=(tc - 1 == 0), stop=(tc - 1 == TC - 1))
                        sm.tag(ins, "t3kv")
                W(t, "v3pk", TC * (fc + 1))
                for hh in range(2):
                    ins = t.matmul(psB[0:65, 512 * hh:512 * hh + 256],
                                   vaug[:, TC - 1, :].rearrange("p (h w) -> p h w", w=65)[:, 2 * fc + hh, :],
                                   pk[:, TC - 1, 256 * hh:256 * (hh + 1)], start=False, stop=True)
                sm.tag(ins, "t3kv")
                W(t, "s3kv", fc + 1)
                if fc >= 1:
                    W(t, "v3kz", fc)
                for j in range(4):
                    ins = t.transpose(psU[:, j, 0:65], kvzTs[0:65, 128 * j:128 * (j + 1)], ident[0:65, 0:65])
                sm.tag(ins, "t3kT")

        @blk.scalar
        def _(s):
            for fc in range(NF):
                for tc in range(TC):
                    gi = TC * fc + tc
                    W(s, "t3p", gi + 1)
                    if gi >= 2:
                        W(s, "v3pk", gi - 1)
                    sl = gi % 4
                    sm.tag(s.activation(pse[:, gi % 2, :], psA[:, 512 * sl:512 * (sl + 1)], AF.Exp), "s3e")
                W(s, "t3kv", TC * (fc + 1))
                sm.tag(s.copy(kvzTs[:].rearrange("p (b c) -> p b c", b=2),
                              psB[0:65, 0:1024].rearrange("p (b c) -> p b c", b=2)[:, :, 0:256]), "s3kv")

        @blk.vector
        def _(v):
            for fc in range(NF):
                for tc in range(TC):
                    gi = TC * fc + tc
                    W(v, "s3e", gi + 1)
                    sm.tag(v.tensor_scalar(pk[:, tc, :], pse[:, gi % 2, :], 1e-6, None, op0=OP.add), "v3pk")
                W(v, "t3kT", fc + 1)
                sm.tag(v.tensor_copy(kvz[:, 4 * fc:4 * (fc + 1), :], psU[:, 0:4, 0:65]), "v3kz")

        @blk.sync
        def _(s):
            W(s, "v3kz", NF)
            dump(s, "kvzd", kvz[:].rearrange("p a b -> p (a b)"), [128, 32 * 65], BF16)

    sb.free("pk", "vaug", "ombd", "indk", "ckT", "kT", "sgl", "up1", "pse", "kvzTs")
    if upto <= 4:
        return fin(nc, sm, dbg_h)

    attT = sb.alloc("attT", [128, NF, N], BF16)
    pqT = sb.alloc("pqT", [128, 2, 2, N], BF16)
    numTs = sb.alloc("numTs", [64, 2, N], F32)
    denb = sb.alloc("denb", [65, 2, N], BF16)
    lnv = sb.alloc("lnv", [65, 2, 1024], F32)
    pse4 = sb.alloc("pse4", [128, 4, N], BF16)
    cone = sb.alloc("cone", [128, 64], BF16)
    wob = sb.alloc("wob", [128, NF, D], BF16)

    with nc.Block("p4b", no_gpsimd_drain=True) as blk:
        @blk.gpsimd
        def _(g):
            tagev(g.dma_start(wob[:], dwo.rearrange("(c p) f -> p c f", p=128)), "dW4", ("wo",))

        @blk.tensor
        def _(t):
            W(t, "v4a", 1)

            def ARG(h):
                fc, hh = h // 2, h % 2
                for mc in range(2):
                    gi = 2 * h + mc
                    for j in range(2):
                        if gi >= 1:
                            W(t, "s4e", 2 * (gi - 1) + j + 1)
                        for pc in (2 * j, 2 * j + 1):
                            t.matmul(psA[:, 512 * pc:512 * (pc + 1)],
                                     omTD[64 * hh:64 * hh + 64, 128 * mc:128 * (mc + 1)],
                                     qT[64 * hh:64 * hh + 64, fc, 512 * pc:512 * (pc + 1)], start=True, stop=False)
                            ins = t.matmul(psA[:, 512 * pc:512 * (pc + 1)], indq[:, 128 * h:128 * (h + 1)],
                                           cqT[:, 512 * pc:512 * (pc + 1)], start=False, stop=True)
                        sm.tag(ins, "t4p")

            def NUMB(h):
                for half in range(2):
                    if half == 0:
                        W(t, "v4pq", 2 * h + 2)
                        if h >= 1:
                            W(t, "v4n", 2 * h)
                            W(t, "s4n", 2 * h)
                    else:
                        W(t, "v4n", 2 * h + 1)
                        W(t, "s4n", 2 * h + 1)
                    for pc in range(2):
                        for mc in range(2):
                            ins = t.matmul(psB[0:65, 512 * pc:512 * (pc + 1)], kvz[:, 2 * h + mc, :],
                                           pqT[:, h % 2, mc, 1024 * half + 512 * pc:1024 * half + 512 * (pc + 1)],
                                           start=(mc == 0), stop=(mc == 1))
                    sm.tag(ins, "t4n")
                for half in range(2):
                    W(t, "s4n", 2 * h + half + 1)
                    if h >= 1 or half == 1:
                        W(t, "v4a", 2 * h + half + 1)
                    t.matmul(psD[0:64, :], cone[64:65, 0:64],
                             denb[64:65, h % 2, 1024 * half:1024 * half + 512], start=True, stop=True)
                    ins = t.matmul(psE[0:64, :], cone[64:65, 0:64],
                                   denb[64:65, h % 2, 1024 * half + 512:1024 * (half + 1)], start=True, stop=True)
                    sm.tag(ins, "t4b")

            for step in range(H + 1):
                if step < H:
                    ARG(step)
                if step >= 1:
                    NUMB(step - 1)

        @blk.scalar
        def _(s):
            def den(h, half):
                if half == 0 and h >= 2:
                    W(s, "t4b", 2 * h - 2)
                W(s, "t4n", 2 * h + half + 1)
                sm.tag(s.activation(lnv[64:65, 0, :], psB[64:65, 0:1024], AF.Ln), "sch")
                W(s, "sch")
                sm.tag(s.activation(denb[64:65, h % 2, 1024 * half:1024 * (half + 1)],
                                    lnv[64:65, 0, :], AF.Exp, scale=-1.0), "s4n")

            for step in range(H + 1):
                if step < H:
                    h = step
                    for mc in range(2):
                        gi = 2 * h + mc
                        for j in range(2):
                            W(s, "t4p", 2 * gi + j + 1)
                            if gi >= 4 and j == 0:
                                W(s, "v4pq", gi - 3)
                            sm.tag(s.activation(pse4[:, gi % 4, 1024 * j:1024 * (j + 1)],
                                                psA[:, 1024 * j:1024 * (j + 1)], AF.Exp), "s4e")
                if step >= 1:
                    for half in range(2):
                        den(step - 1, half)

        @blk.vector
        def _(v):
            sm.tag(v.memset(cone[:], 1.0), "v4a")
            for step in range(H + 1):
                if step < H:
                    h = step
                    if h >= 2:
                        W(v, "t4n", 2 * (h - 1))
                    for mc in range(2):
                        gi = 2 * h + mc
                        W(v, "s4e", 2 * gi + 2)
                        sm.tag(v.tensor_scalar(pqT[:, h % 2, mc, :], pse4[:, gi % 4, :], 1e-6, None, op0=OP.add), "v4pq")
                if step >= 1:
                    h = step - 1
                    fc, hh = h // 2, h % 2
                    for half in range(2):
                        W(v, "t4n", 2 * h + half + 1)
                        sm.tag(v.tensor_copy(numTs[0:64, h % 2, 1024 * half:1024 * (half + 1)], psB[0:64, :]), "v4n")
                        W(v, "t4b", 2 * h + half + 1)
                        v.tensor_tensor(attT[64 * hh:64 * hh + 64, fc, 1024 * half:1024 * half + 512],
                                        numTs[0:64, h % 2, 1024 * half:1024 * half + 512],
                                        psD[0:64, :], op=OP.mult)
                        sm.tag(v.tensor_tensor(attT[64 * hh:64 * hh + 64, fc, 1024 * half + 512:1024 * (half + 1)],
                                               numTs[0:64, h % 2, 1024 * half + 512:1024 * (half + 1)],
                                               psE[0:64, :], op=OP.mult), "v4a")

        @blk.sync
        def _(s):
            W(s, "v4a", 2 * H + 1)
            dump(s, "attTd", attT[:].rearrange("p c n -> p (c n)"), [128, NF * N], BF16)

    sb.free("pqT", "numTs", "denb", "lnv", "pse4", "cone", "kvz", "qT", "omTs", "omTD", "indq", "cqT")
    if upto <= 5:
        return fin(nc, sm, dbg_h)

    vsml = sb.alloc("vsml", [1, D], F32)
    ln1gb = sb.alloc("ln1gb", [128, D], F32)
    ln1bb = sb.alloc("ln1bb", [128, D], F32)
    ln2gb = sb.alloc("ln2gb", [128, D], F32)
    ln2bb = sb.alloc("ln2bb", [128, D], F32)
    b2b = sb.alloc("b2b", [128, D], F32)

    xt2 = sb.alloc("xt2", [128, 3, D], F32)
    scr1 = sb.alloc("scr1", [128, D], F32)
    scr2 = sb.alloc("scr2", [128, D], F32)
    x1t = sb.alloc("x1t", [128, 2, D], F32)
    x1bt = sb.alloc("x1bt", [128, 2, D], BF16)
    x1Ts = sb.alloc("x1Ts", [128, 2, NF, 128], BF16)
    w1bs = [sb.alloc("w1b%d" % i, [128, 2, FF], BF16) for i in range(4)]

    with nc.Block("p5", no_gpsimd_drain=True) as blk:
        @blk.sync
        def _(s):
            for i, (dv, tgt) in enumerate(((dln1g, ln1gb), (dln1b, ln1bb), (dln2g, ln2gb), (dln2b, ln2bb), (db2, b2b))):
                if i >= 1:
                    W(s, "t5v", i)
                tagev(s.dma_start(vsml[:], dv[None, :]), "d5v", ("v5", i))

        @blk.tensor
        def _(t):
            for i in range(5):
                Wev(t, ("v5", i))
                if i >= 1:
                    W(t, "v5v", i)
                for sl in range(2):
                    ins = t.matmul(psB[:, 512 * sl:512 * (sl + 1)], onesf[:], vsml[0:1, 512 * sl:512 * (sl + 1)],
                                   start=True, stop=True)
                sm.tag(ins, "t5v")

        @blk.vector
        def _(v):
            for i, tgt in enumerate((ln1gb, ln1bb, ln2gb, ln2bb, b2b)):
                W(v, "t5v", i + 1)
                sm.tag(v.tensor_copy(tgt[:], psB[:]), "v5v")

        @blk.gpsimd
        def _(g):
            W(g, "dO")
            for dc in range(NF):
                tagev(g.dma_start(w1bs[dc // 2][:, dc % 2, :], dw1[128 * dc:128 * (dc + 1), :]), "dW1", ("w1",))

        @blk.sync
        def _(s):
            def spill(tc):
                W(s, "v5x1", 2 * (tc + 1))
                tagev(s.dma_start(dx1spill[128 * tc:128 * (tc + 1), :], x1t[:, tc % 2, :]), "dsp%d" % (tc % 2), ("sp", tc))
                W(s, "s5ev", tc + 1)
                tagev(s.dma_start(dx1T[:, :, 128 * tc:128 * (tc + 1)].rearrange("c p f -> p c f"),
                                  x1Ts[:, tc % 2]), "dspT%d" % (tc % 2), ("spT", tc))
            for tc in range(TC):
                if tc >= 3:
                    W(s, "v5pre", tc - 2)
                tagev(s.dma_start(xt2[:, tc % 3, :], dx[128 * tc:128 * (tc + 1), :]), "d5x%d" % (tc % 3), ("x5", tc))
                if tc >= 2:
                    spill(tc - 2)
            spill(TC - 2)
            spill(TC - 1)
            for key in (("sp", TC - 2), ("sp", TC - 1), ("spT", TC - 2), ("spT", TC - 1)):
                Wev(s, key)

        @blk.tensor
        def _(t):
            def tr5(g):
                W(t, "v5x1", 2 * (g + 1))
                if g >= 2:
                    W(t, "s5ev", g - 1)
                pT = psT if g % 2 == 0 else psU
                for c in range(NF):
                    ins = t.transpose(pT[:, c, 0:128], x1bt[:, g % 2, 128 * c:128 * (c + 1)], ident[:])
                sm.tag(ins, "t5tr")

            Wev(t, ("wo",))
            for tc in range(TC):
                if tc >= 2:
                    W(t, "v5pre", tc - 1)
                for fc in range(NF):
                    for half in range(2):
                        ins = t.matmul(psA[:, 1024 * (tc % 2) + 512 * half:1024 * (tc % 2) + 512 * (half + 1)],
                                       attT[:, fc, 128 * tc:128 * (tc + 1)],
                                       wob[:, fc, 512 * half:512 * (half + 1)], start=(fc == 0), stop=(fc == NF - 1))
                sm.tag(ins, "t5wo")
                if tc >= 1:
                    tr5(tc - 1)
            tr5(TC - 1)

        @blk.vector
        def _(v):
            for tc in range(TC):
                W(v, "t5wo", tc + 1)
                Wev(v, ("x5", tc))
                if tc >= 2:
                    W(v, "t5tr", tc - 1)
                v.tensor_scalar(scr1[:], xt2[:, tc % 3, :], mvs[:, tc, 0:1], r0s[:, tc:tc + 1],
                                op0=OP.subtract, op1=OP.mult)
                sm.tag(v.tensor_tensor(scr2[:], psA[:, 1024 * (tc % 2):1024 * (tc % 2) + 1024],
                                       scr1[:], op=OP.add), "v5pre")
                v.bn_stats(bns[:, 0, :], scr2[:, 0:512])
                sm.tag(v.bn_stats(bns[:, 1, :], scr2[:, 512:1024]), "vch")
                W(v, "vch")
                sm.tag(v.bn_aggr(mvx[:], bns[:].rearrange("p a b -> p (a b)")), "v5var")
                W(v, "s5r", tc + 1)
                v.scalar_tensor_tensor(scr1[:], scr2[:], mvx[:, 0:1], ln1gb[:],
                                       op0=OP.subtract, op1=OP.mult)
                v.tensor_scalar(scr1[:], scr1[:], rv[:, 1:2], None, op0=OP.mult)
                if tc >= 2:
                    Wev(v, ("sp", tc - 2))
                sm.tag(v.tensor_tensor(x1t[:, tc % 2, :], scr1[:], ln1bb[:], op=OP.add), "v5x1")
                sm.tag(v.tensor_copy(x1bt[:, tc % 2, :], x1t[:, tc % 2, :]), "v5x1")

        @blk.scalar
        def _(s):
            for tc in range(TC):
                W(s, "v5var", tc + 1)
                sm.tag(s.activation(rv[:, 4:5], mvx[:, 1:2], AF.Ln, bias=epsv1[:]), "sch")
                W(s, "sch")
                sm.tag(s.activation(rv[:, 1:2], rv[:, 4:5], AF.Exp, scale=-0.5), "s5r")
                W(s, "t5tr", tc + 1)
                if tc >= 2:
                    Wev(s, ("spT", tc - 2))
                pT = psT if tc % 2 == 0 else psU
                sm.tag(s.copy(x1Ts[:, tc % 2], pT[:, 0:NF, 0:128]), "s5ev")

    sb.free("attT", "wob", "xt2", "scr1", "x1bt", "x1t", "x1Ts", "ln1gb", "ln1bb", "vsml")
    if upto <= 6:
        return fin(nc, sm, dbg_h)

    w2bs = [sb.alloc("w2b%d" % i, [128, 8, D], BF16) for i in range(4)]
    x1Tp = sb.alloc("x1Tp", [128, 2, NF, 256], BF16)
    h1p = sb.alloc("h1p", [128, 32, 256], BF16)
    x1r = sb.alloc("x1r", [128, 2, D], F32)
    gr = sb.alloc("gr", [128, 2, D], F32)
    xr0 = sb.alloc("xr0", [128, D], F32)
    xr1 = sb.alloc("xr1", [128, D], F32)

    with nc.Block("p6", no_gpsimd_drain=True) as blk:
        @blk.gpsimd
        def _(g):
            W(g, "dO")
            for key in (("sp", TC - 2), ("sp", TC - 1), ("spT", TC - 2), ("spT", TC - 1)):
                Wev(g, key)
            for i in range(4):
                tagev(g.dma_start(w2bs[i][:],
                                  dw2[1024 * i:1024 * (i + 1), :].rearrange("(c p) f -> p c f", p=128)), "dW2b", ("w2b",))

        @blk.sync
        def _(s):
            for key in (("g1", TC - 2), ("g1", TC - 1), ("sp", TC - 2), ("sp", TC - 1),
                        ("spT", TC - 2), ("spT", TC - 1)):
                Wev(s, key)
            for p in range(8):
                if p >= 2:
                    W(s, "t6w1", 32 * (p - 1))
                tagev(s.dma_start(x1Tp[:, p % 2],
                                  dx1T[:, :, 256 * p:256 * (p + 1)].rearrange("c p f -> p c f")), "d6xT%d" % (p % 2), ("xT6", p))
                for tcl in range(2):
                    tc = 2 * p + tcl
                    if tc >= 2:
                        W(s, "v6fin", tc - 1)
                        Wev(s, ("oo", tc - 2))
                    tagev(s.dma_start(x1r[:, tc % 2, :], dx1spill[128 * tc:128 * (tc + 1), :]), "d6a%d" % (tc % 2), ("r6a", tc))
                    tagev(s.dma_start(gr[:, tc % 2, :], dgspill[128 * tc:128 * (tc + 1), :]), "d6b%d" % (tc % 2), ("r6b", tc))
                    tagev(s.dma_start((xr0 if tc % 2 == 0 else xr1)[:], dx[128 * tc:128 * (tc + 1), :]), "d6c%d" % (tc % 2), ("r6c", tc))
                    if tc >= 1:
                        W(s, "v6fin", tc)
                        tagev(s.dma_start(dout[128 * (tc - 1):128 * tc, :], x1r[:, (tc - 1) % 2, :]), "dOo%d" % ((tc - 1) % 2), ("oo", tc - 1))
            W(s, "v6fin", TC)
            tagev(s.dma_start(dout[128 * (TC - 1):128 * TC, :], x1r[:, (TC - 1) % 2, :]), "dOo%d" % ((TC - 1) % 2), ("oo", TC - 1))

        @blk.tensor
        def _(t):
            Wev(t, ("w1",))
            for p in range(8):
                Wev(t, ("xT6", p))
                for fk in range(32):
                    gi = 32 * p + fk
                    if gi >= 4:
                        W(t, "s6g", gi - 3)
                    sl = fk % 4
                    for dc in range(NF):
                        ins = t.matmul(psA[:, 512 * sl:512 * sl + 256],
                                       w1bs[dc // 2][:, dc % 2, 128 * fk:128 * (fk + 1)],
                                       x1Tp[:, p % 2, dc, :], start=(dc == 0), stop=(dc == NF - 1))
                    sm.tag(ins, "t6w1")
                if p == 0:
                    Wev(t, ("w2b",))
                for tcl in range(2):
                    tc = 2 * p + tcl
                    if tc >= 1:
                        W(t, "v6pre", tc)
                    for fk in range(32):
                        if tcl == 0:
                            W(t, "s6g", 32 * p + fk + 1)
                        for half in range(2):
                            ins = t.matmul(psB[:, 512 * half:512 * (half + 1)],
                                           h1p[:, fk, 128 * tcl:128 * (tcl + 1)],
                                           w2bs[fk // 8][:, fk % 8, 512 * half:512 * (half + 1)], start=(fk == 0), stop=(fk == 31))
                    sm.tag(ins, "t6w2")

        @blk.scalar
        def _(s):
            for p in range(8):
                for fk in range(32):
                    gi = 32 * p + fk
                    W(s, "t6w1", gi + 1)
                    sl = fk % 4
                    sm.tag(s.activation(h1p[:, fk, :], psA[:, 512 * sl:512 * sl + 256], AF.Gelu,
                                        bias=b1s[:, fk:fk + 1]), "s6g")
                for tcl in range(2):
                    tc = 2 * p + tcl
                    W(s, "v6var", tc + 1)
                    sm.tag(s.activation(rv[:, 4:5], mvx[:, 1:2], AF.Ln, bias=epsv1[:]), "sch")
                    W(s, "sch")
                    sm.tag(s.activation(rv[:, 1:2], rv[:, 4:5], AF.Exp, scale=-0.5), "s6r")

        @blk.vector
        def _(v):
            for p in range(8):
                for tcl in range(2):
                    tc = 2 * p + tcl
                    W(v, "t6w2", tc + 1)
                    Wev(v, ("r6a", tc))
                    Wev(v, ("r6b", tc))
                    Wev(v, ("r6c", tc))
                    if tc >= 2:
                        Wev(v, ("oo", tc - 2))
                    v.tensor_tensor(scr2[:], psB[:], b2b[:], op=OP.add)
                    sm.tag(v.tensor_tensor(scr2[:], scr2[:], x1r[:, tc % 2, :], op=OP.add), "v6pre")
                    v.bn_stats(bns[:, 0, :], scr2[:, 0:512])
                    sm.tag(v.bn_stats(bns[:, 1, :], scr2[:, 512:1024]), "vch")
                    W(v, "vch")
                    sm.tag(v.bn_aggr(mvx[:], bns[:].rearrange("p a b -> p (a b)")), "v6var")
                    W(v, "s6r", tc + 1)
                    v.scalar_tensor_tensor(sqv[:], scr2[:], mvx[:, 0:1], ln2gb[:],
                                           op0=OP.subtract, op1=OP.mult)
                    v.tensor_scalar(sqv[:], sqv[:], rv[:, 1:2], None, op0=OP.mult)
                    v.tensor_tensor(sqv[:], sqv[:], ln2bb[:], op=OP.add)
                    if "rawffn" in debug:
                        sm.tag(v.tensor_copy(x1r[:, tc % 2, :], scr2[:]), "v6fin")
                    else:
                        v.tensor_tensor(sqv[:], sqv[:], gr[:, tc % 2, :], op=OP.mult)
                        sm.tag(v.tensor_tensor(x1r[:, tc % 2, :], sqv[:], (xr0 if tc % 2 == 0 else xr1)[:], op=OP.add), "v6fin")

    return fin(nc, sm, dbg_h)


def fin(nc, sm, dbg_h):
    with nc.Block("fin", no_gpsimd_drain=True) as blk:
        @blk.sync
        def _(s):
            for name in sorted(sm.v):
                if name.startswith("d") and sm.v[name] > 0:
                    s.wait_ge(sm.get(name), sm.v[name])
    return nc, dbg_h





def _in_map(ins, core):
    A = lambda a: np.ascontiguousarray(np.asarray(a, np.float32))
    return {
        "x": A(ins["L0_attn_x"][core]),
        "wq": A(np.asarray(ins["L0_attn_wq"], np.float32).reshape(D, D)),
        "wk": A(np.asarray(ins["L0_attn_wk"], np.float32).reshape(D, D)),
        "wv": A(np.asarray(ins["L0_attn_wv"], np.float32).reshape(D, D)),
        "wo": A(ins["L0_attn_wo"]),
        "omT": A(np.asarray(ins["L0_attn_omega"], np.float32).T),
        "ln1g": A(ins["L0_attn_ln1_g"]), "ln1b": A(ins["L0_attn_ln1_b"]),
        "w1": A(ins["L0_attn_w1"]), "b1": A(ins["L0_attn_b1"]),
        "w2": A(ins["L0_attn_w2"]), "b2": A(ins["L0_attn_b2"]),
        "ln2g": A(ins["L0_attn_ln2_g"]), "ln2b": A(ins["L0_attn_ln2_b"]),
        "Wr": A(ins["L0_ffn_Wr"]), "Wg": A(ins["L0_ffn_Wg"]), "Wu": A(ins["L0_ffn_Wu"]),
        "Wd": A(ins["L0_ffn_Wd"]), "Sg": A(ins["L0_ffn_Sg"]), "Su": A(ins["L0_ffn_Su"]),
        "Sd": A(ins["L0_ffn_Sd"]), "eb": A(ins["L0_ffn_expert_bias"]),
    }


def kernel(**inputs):
    ncores = 8
    nc, _ = build()
    in_maps = [_in_map(inputs, c) for c in range(ncores)]
    res = run_bass_kernel_spmd(nc, in_maps, core_ids=list(range(ncores)))
    return np.stack([np.asarray(res.results[c]["out"], np.float32) for c in range(ncores)], axis=0)


# version 23 (final)
# speedup vs baseline: 1.0108x; 1.0108x over previous
import math
import numpy as np
import concourse.bass as bass
import concourse.mybir as mybir
from concourse.alu_op_type import AluOpType as OP
from concourse.bass_utils import run_bass_kernel_spmd

F32 = mybir.dt.float32
BF16 = mybir.dt.bfloat16
AF = mybir.ActivationFunctionType
AX = mybir.AxisListType

N, TC, D, NF, H, KH, M, FF, E, FEXP, FS = 2048, 16, 1024, 8, 16, 64, 256, 4096, 8, 512, 1024
EPS0, EPS1 = 1e-5, 1e-6
SCALE = float(KH) ** (-0.25)
LNRTM = math.log(math.sqrt(M))
SQC = 0.5 * SCALE * SCALE


class SB:
    ALIGN = 64

    def __init__(self, nc, nbytes):
        self.nc = nc
        self.arena = nc.alloc_sbuf_tensor("arena", [128, nbytes], mybir.dt.uint8)
        self.base = nc.lookup_mloc(self.arena).addr
        self.freel = [[0, nbytes]]
        self.live = {}
        self.peak = 0

    def alloc(self, name, shape, dtype):
        import functools, operator
        per = functools.reduce(operator.mul, list(shape[1:]), 1) * mybir.dt.size(dtype)
        size = (per + self.ALIGN - 1) // self.ALIGN * self.ALIGN
        for ent in self.freel:
            if ent[1] >= size:
                st = ent[0]
                ent[0] += size
                ent[1] -= size
                h = self.nc.virtual_sbuf_tensor(name, list(shape), dtype)
                self.nc._tensor(h.name, list(shape), dtype, type="SB")
                m = self.nc.lookup_mloc(h)
                m.allocated = True
                m.addr = self.base + st
                self.live[name] = (st, size)
                self.peak = max(self.peak, sum(s for _, s in self.live.values()))
                return h
        raise MemoryError(f"sbuf arena OOM for {name} ({size}B/part): free={self.freel}, live={ {k: v[1] for k, v in self.live.items()} }")

    def free(self, *names):
        for n in names:
            st, size = self.live.pop(n)
            self.freel.append([st, size])
        self.freel.sort()
        out = []
        for st, sz in self.freel:
            if sz == 0:
                continue
            if out and out[-1][0] + out[-1][1] == st:
                out[-1][1] += sz
            else:
                out.append([st, sz])
        self.freel = out


SEM_ALIAS = {
    "d5x0": "dx0", "d5x1": "dx1", "d5x2": "dx2",
    "d6a0": "dx0", "d6a1": "dx1", "d6b0": "dx2", "d6b1": "dx3",
    "d6c0": "dXc", "d6c1": "d5v",
    "d6xT0": "dt20", "d6xT1": "dt21",
    "dW1": "dWg", "dW2b": "dWs", "dW4": "dwd1",
    "dsw0": "dgu0", "dsw1": "dgu1", "dsw2": "dwd0",
    "g3io": "gio", "v3om": "v3ic", "v0idf": "v0c", "v0b1": "v0c", "v0lg": "v0rwT",
    "t0b1": "t0rw", "t0lg": "t0rw", "t0lt": "t0rw", "s3ic": "s0c", "v2i": "v0rwT",
}


class Sems:
    def __init__(self, nc):
        self.nc = nc
        self.h = {}
        self.v = {}

    def get(self, name):
        name = SEM_ALIAS.get(name, name)
        if name not in self.h:
            self.h[name] = self.nc.alloc_semaphore(name)
            self.v[name] = 0
        return self.h[name]

    def val(self, name):
        name = SEM_ALIAS.get(name, name)
        self.get(name)
        return self.v[name]

    def tag(self, instr, name, n=1):
        name = SEM_ALIAS.get(name, name)
        instr.then_inc(self.get(name), n)
        self.v[name] += n
        return self.v[name]


def build(debug=(), upto=99):
    nc = bass.Bass("TRN2", target_bir_lowering=False)
    nc.detect_race_conditions = False
    sb = None
    sm = Sems(nc)

    def W(eng, name, target=None):
        eng.wait_ge(sm.get(name), sm.val(name) if target is None else target)

    ev = {}

    def tagev(instr, sem_name, key, n=16):
        v = sm.tag(instr, sem_name, n)
        ev[key] = (sem_name, v)

    def Wev(eng, key):
        nm, v = ev[key]
        eng.wait_ge(sm.get(nm), v)

    dx = nc.dram_tensor("x", [N, D], F32, kind="ExternalInput").ap()
    dwq = nc.dram_tensor("wq", [D, D], F32, kind="ExternalInput").ap()
    dwk = nc.dram_tensor("wk", [D, D], F32, kind="ExternalInput").ap()
    dwv = nc.dram_tensor("wv", [D, D], F32, kind="ExternalInput").ap()
    dwo = nc.dram_tensor("wo", [D, D], F32, kind="ExternalInput").ap()
    domT = nc.dram_tensor("omT", [KH, M], F32, kind="ExternalInput").ap()
    dln1g = nc.dram_tensor("ln1g", [D], F32, kind="ExternalInput").ap()
    dln1b = nc.dram_tensor("ln1b", [D], F32, kind="ExternalInput").ap()
    dw1 = nc.dram_tensor("w1", [D, FF], F32, kind="ExternalInput").ap()
    db1 = nc.dram_tensor("b1", [FF], F32, kind="ExternalInput").ap()
    dw2 = nc.dram_tensor("w2", [FF, D], F32, kind="ExternalInput").ap()
    db2 = nc.dram_tensor("b2", [D], F32, kind="ExternalInput").ap()
    dln2g = nc.dram_tensor("ln2g", [D], F32, kind="ExternalInput").ap()
    dln2b = nc.dram_tensor("ln2b", [D], F32, kind="ExternalInput").ap()
    dWr = nc.dram_tensor("Wr", [D, E], F32, kind="ExternalInput").ap()
    dWg = nc.dram_tensor("Wg", [E, D, FEXP], F32, kind="ExternalInput").ap()
    dWu = nc.dram_tensor("Wu", [E, D, FEXP], F32, kind="ExternalInput").ap()
    dWd = nc.dram_tensor("Wd", [E, FEXP, D], F32, kind="ExternalInput").ap()
    dSg = nc.dram_tensor("Sg", [D, FS], F32, kind="ExternalInput").ap()
    dSu = nc.dram_tensor("Su", [D, FS], F32, kind="ExternalInput").ap()
    dSd = nc.dram_tensor("Sd", [FS, D], F32, kind="ExternalInput").ap()
    deb = nc.dram_tensor("eb", [E], F32, kind="ExternalInput").ap()

    dout = nc.dram_tensor("out", [N, D], F32, kind="ExternalOutput").ap()
    spill_kind = lambda nm: "ExternalOutput" if nm in debug else "Internal"
    dgspill = nc.dram_tensor("gspill", [N, D], F32, kind=spill_kind("gspill")).ap()
    dx1spill = nc.dram_tensor("x1spill", [N, D], F32, kind=spill_kind("x1spill")).ap()
    dx1T = nc.dram_tensor("x1T", [NF, 128, N], BF16, kind=spill_kind("x1T")).ap()

    dbg_h = {}

    def dump(s, name, src_ap, shape, dtype=F32):
        if name in debug:
            h = nc.dram_tensor("dbg_" + name, list(shape), dtype, kind="ExternalOutput").ap()
            dbg_h[name] = h
            sm.tag(s.dma_start(h, src_ap), "dO", 16)

    psA = nc.alloc_psum_tensor("psA", [128, 2048], F32)
    psB = nc.alloc_psum_tensor("psB", [128, 1024], F32)
    psT = nc.alloc_psum_tensor("psT", [128, 8, 128], BF16)
    psU = nc.alloc_psum_tensor("psU", [128, 8, 128], BF16)

    def psum_alias(name, bank):
        hnd = nc.virtual_psum_tensor(name, [128, 512], F32)
        nc._tensor(hnd.name, [128, 512], F32, type="PSUM")
        m = nc.lookup_mloc(hnd)
        m.allocated = True
        m.bank = bank
        return hnd
    psD = psum_alias("psD", 6)
    psE = psum_alias("psE", 7)

    ident = nc.alloc_sbuf_tensor("ident", [128, 128], BF16)
    ones1 = nc.alloc_sbuf_tensor("ones1", [1, 128], BF16)
    onesf = nc.alloc_sbuf_tensor("onesf", [1, 128], F32)
    iop = nc.alloc_sbuf_tensor("iop", [128, 128], F32)
    iof = nc.alloc_sbuf_tensor("iof", [128, 128], F32)
    io32 = nc.alloc_sbuf_tensor("io32", [32, 1], F32)
    r0s = nc.alloc_sbuf_tensor("r0s", [128, TC], F32)
    sqv = nc.alloc_sbuf_tensor("sqv", [128, 1024], F32)
    epsv0 = nc.alloc_sbuf_tensor("epsv0", [128, 1], F32)
    epsv1 = nc.alloc_sbuf_tensor("epsv1", [128, 1], F32)
    bns = nc.alloc_sbuf_tensor("bns", [128, 2, 6], F32)
    mvs = nc.alloc_sbuf_tensor("mvs", [128, TC, 2], F32)
    mvx = nc.alloc_sbuf_tensor("mvx", [128, 2], F32)
    b1s = nc.alloc_sbuf_tensor("b1s", [128, FF // 128], F32)
    b1stg = nc.alloc_sbuf_tensor("b1stg", [FF // 128, 128], F32)
    identf = nc.alloc_sbuf_tensor("identf", [128, 128], F32)
    ebs = nc.alloc_sbuf_tensor("ebs", [E, 1], F32)
    cbf = nc.alloc_sbuf_tensor("cbf", [128, TC * H], F32)
    cb32 = nc.alloc_sbuf_tensor("cb32", [128, TC, 2 * H], BF16)
    rv = nc.alloc_sbuf_tensor("rv", [128, 16], F32)
    rtm = nc.alloc_sbuf_tensor("rtm", [128, 4, 16], F32)
    rtb = nc.alloc_sbuf_tensor("rtb", [128, 2, 128], F32)
    rva = nc.alloc_sbuf_tensor("rva", [128, 8], F32)
    rvb = nc.alloc_sbuf_tensor("rvb", [128, 8], F32)

    sb = SB(nc, 201664)
    ind8 = sb.alloc("ind8", [8, 1024], BF16)
    rwT = sb.alloc("rwT", [8, N], BF16)
    xnT = sb.alloc("xnT", [128, NF, N], BF16)
    xnloT = sb.alloc("xnloT", [128, NF, N], BF16)
    xt = sb.alloc("xt", [128, 4, D], F32)
    xn32 = sb.alloc("xn32", [128, 2, D], F32)
    xnb = sb.alloc("xnb", [128, 2, D], BF16)
    xnlo = sb.alloc("xnlo", [128, 2, D], BF16)
    wrh = sb.alloc("wrh", [128, NF, E], BF16)
    wrl = sb.alloc("wrl", [128, NF, E], BF16)
    wrf = sb.alloc("wrf", [128, NF, E], F32)
    lgT = sb.alloc("lgT", [8, N], F32)
    rwt = sb.alloc("rwt", [128, TC, 8], F32)
    rwb16 = sb.alloc("rwb16", [128, TC, 8], BF16)

    with nc.Block("p0", no_gpsimd_drain=True) as blk:
        @blk.sync
        def _(s):
            sm.tag(s.dma_start(wrf[:], dWr.rearrange("(c p) e -> p c e", p=128)), "dXc", 16)
            sm.tag(s.dma_start(ebs[:], deb[:, None]), "dXc", 16)
            tagev(s.dma_start(b1stg[:], db1.rearrange("(c f) -> c f", f=128)), "dXc", ("xc",))
            for tc in range(TC):
                if tc >= 4:
                    W(s, "v0xn", tc - 3)
                tagev(s.dma_start(xt[:, tc % 4, :], dx[128 * tc:128 * (tc + 1), :]), "dx%d" % (tc % 4), ("x0", tc))

        @blk.gpsimd
        def _(g):
            g.iota(iop[:], pattern=[[0, 128]], channel_multiplier=1, allow_small_or_imprecise_dtypes=True)
            g.iota(iof[:], pattern=[[1, 128]], channel_multiplier=0, allow_small_or_imprecise_dtypes=True)
            g.iota(io32[:], pattern=[[0, 1]], channel_multiplier=1, allow_small_or_imprecise_dtypes=True)
            sm.tag(g.iota(ind8[:], pattern=[[1, E], [0, 128]], channel_multiplier=0, allow_small_or_imprecise_dtypes=True), "gio")

        @blk.vector
        def _(v):
            W(v, "gio")
            v.tensor_tensor(ident[:], iop[:], iof[:], op=OP.is_equal)
            v.memset(ones1[:], 1.0)
            v.memset(onesf[:], 1.0)
            v.tensor_scalar(rv[0:32, 0:1], io32[:], -1.0, 0.5, op0=OP.mult, op1=OP.add)
            v.tensor_scalar(rv[0:32, 1:2], io32[:], -1.0, None, op0=OP.mult)
            v.memset(rv[0:32, 9:10], -1.0)
            v.memset(epsv0[:], EPS0)
            v.memset(epsv1[:], EPS1)
            sm.tag(v.memset(rv[0:32, 10:11], -0.5), "v0c")

        @blk.scalar
        def _(s):
            W(s, "v0c")
            sm.tag(s.activation(ind8[:], ind8[:], AF.Abs, bias=rv[0:8, 1:2], scale=1.0), "sch")
            W(s, "sch")
            sm.tag(s.activation(ind8[:], ind8[:], AF.Sign, bias=rv[0:8, 10:11], scale=1.0), "s0c")

        @blk.vector
        def _(v):
            W(v, "s0c")
            v.tensor_scalar(ind8[:], ind8[:], -0.5, 0.5, op0=OP.mult, op1=OP.add)
            Wev(v, ("xc",))
            v.tensor_copy(wrh[:].rearrange("p c e -> p (c e)"), wrf[:].rearrange("p c e -> p (c e)"))
            sm.tag(v.tensor_tensor(wrl[:].rearrange("p c e -> p (c e)"), wrf[:].rearrange("p c e -> p (c e)"),
                                   wrh[:].rearrange("p c e -> p (c e)"), op=OP.subtract), "v0c")

        @blk.scalar
        def _(s):
            for tc in range(TC):
                W(s, "v0var", tc + 1)
                sm.tag(s.activation(rv[:, 4:5], mvs[:, tc, 1:2], AF.Ln, bias=epsv0[:]), "sch")
                W(s, "sch")
                sm.tag(s.activation(r0s[:, tc:tc + 1], rv[:, 4:5], AF.Exp, scale=-0.5), "s0r")
                W(s, "v0x32", tc + 1)
                if tc >= 2:
                    W(s, "t0h", tc - 1)
                sm.tag(s.copy(xnb[:, tc % 2, :], xn32[:, tc % 2, :]), "s0xb")
                W(s, "t0h", tc + 1)
                sm.tag(s.copy(xnT[:, :, 128 * tc:128 * (tc + 1)], psT[:]), "v0evh")
                W(s, "t0l", tc + 1)
                sm.tag(s.copy(xnloT[:, :, 128 * tc:128 * (tc + 1)], psU[:]), "v0evl")

        @blk.vector
        def _(v):
            for tc in range(TC):
                Wev(v, ("x0", tc))
                v.bn_stats(bns[:, 0, :], xt[:, tc % 4, 0:512])
                sm.tag(v.bn_stats(bns[:, 1, :], xt[:, tc % 4, 512:1024]), "vch")
                W(v, "vch")
                sm.tag(v.bn_aggr(mvs[:, tc, :], bns[:].rearrange("p a b -> p (a b)")), "v0var")
                W(v, "s0r", tc + 1)
                if tc >= 2:
                    W(v, "s0xb", tc - 1)
                sm.tag(v.tensor_scalar(xn32[:, tc % 2, :], xt[:, tc % 4, :], mvs[:, tc, 0:1],
                                       r0s[:, tc:tc + 1], op0=OP.subtract, op1=OP.mult), "v0x32")
                W(v, "s0xb", tc + 1)
                if tc >= 2:
                    W(v, "t0l", tc - 1)
                sm.tag(v.tensor_tensor(xnlo[:, tc % 2, :], xn32[:, tc % 2, :], xnb[:, tc % 2, :],
                                       op=OP.subtract), "v0xn")

        @blk.tensor
        def _(t):
            W(t, "v0c")
            for tc in range(TC):
                W(t, "v0xn", tc + 1)
                if tc >= 1:
                    W(t, "v0evh", tc)
                for c in range(NF):
                    ins = t.transpose(psT[:, c, :], xnb[:, tc % 2, 128 * c:128 * (c + 1)], ident[:])
                sm.tag(ins, "t0h")
                if tc >= 1:
                    W(t, "v0evl", tc)
                for c in range(NF):
                    ins = t.transpose(psU[:, c, :], xnlo[:, tc % 2, 128 * c:128 * (c + 1)], ident[:])
                sm.tag(ins, "t0l")

    with nc.Block("p0r", no_gpsimd_drain=True) as blk:
        @blk.vector
        def _(v):
            sm.tag(v.tensor_copy(identf[:], ident[:]), "v0idf")

        @blk.tensor
        def _(t):
            W(t, "v0idf")
            Wev(t, ("xc",))
            sm.tag(t.transpose(psB[:, 0:32], b1stg[:], identf[0:32, 0:32]), "t0b1")

        @blk.vector
        def _(v):
            W(v, "t0b1")
            sm.tag(v.tensor_copy(b1s[:], psB[:, 0:32]), "v0b1")

        @blk.tensor
        def _(t):
            for pc in range(4):
                ops = [(c, lw, rx) for c in range(NF) for (lw, rx) in ((wrh, xnT), (wrl, xnT), (wrh, xnloT))]
                for i, (c, lw, rx) in enumerate(ops):
                    ins = t.matmul(psA[0:8, 512 * pc:512 * (pc + 1)], lw[:, c, :],
                                   rx[:, c, 512 * pc:512 * (pc + 1)], start=(i == 0), stop=(i == len(ops) - 1))
            sm.tag(ins, "t0lg")

        @blk.vector
        def _(v):
            W(v, "t0lg")
            sm.tag(v.tensor_scalar(lgT[:], psA[0:8, :], ebs[:], None, op0=OP.add), "v0lg")

        @blk.tensor
        def _(t):
            W(t, "v0lg")
            W(t, "v0b1")
            for tc in range(TC):
                ins = t.transpose(psB[:, 8 * tc:8 * (tc + 1)], lgT[:, 128 * tc:128 * (tc + 1)], identf[0:8, 0:8])
            sm.tag(ins, "t0lt")

        @blk.vector
        def _(v):
            W(v, "t0lt")
            lg3 = psB[:, 0:128].rearrange("p (g e) -> p g e", e=8)
            def bc(col):
                return rtm[:, col, :].rearrange("p (g one) -> p g one", one=1).broadcast_to([128, TC, 8])
            sm.tag(v.tensor_reduce(rtm[:, 0, :], lg3, axis=AX.X, op=OP.max), "vch")
            W(v, "vch")
            sm.tag(v.tensor_tensor(rtb[:, 0, :].rearrange("p (g e) -> p g e", e=8), lg3, bc(0),
                                   op=OP.subtract), "v0r1")
            W(v, "s0m1", 1)
            v.tensor_scalar(rtb[:, 1, :], rtb[:, 1, :], -1.0, 0.0, op0=OP.mult, op1=OP.max)
            v.tensor_scalar(rtb[:, 1, :], rtb[:, 1, :], 1.0, 1e30, op0=OP.subtract, op1=OP.mult)
            v.tensor_tensor(rtb[:, 1, :], psB[:, 0:128], rtb[:, 1, :], op=OP.add)
            sm.tag(v.tensor_reduce(rtm[:, 1, :], rtb[:, 1, :].rearrange("p (g e) -> p g e", e=8),
                                   axis=AX.X, op=OP.max), "vch")
            W(v, "vch")
            sm.tag(v.tensor_tensor(rtb[:, 1, :].rearrange("p (g e) -> p g e", e=8), lg3, bc(1),
                                   op=OP.subtract), "v0r2")
            sm.tag(v.tensor_tensor(rtm[:, 2, :], rtm[:, 1, :], rtm[:, 0, :], op=OP.subtract), "v0r3")
            W(v, "s0e", 1)
            sm.tag(v.tensor_scalar(rtm[:, 3, :], rtm[:, 2, :], 1.0, None, op0=OP.add), "vch")
            W(v, "vch")
            sm.tag(v.reciprocal(rtm[:, 3, :], rtm[:, 3, :]), "vch")
            W(v, "vch")
            v.tensor_scalar(rtb[:, 1, :], rtb[:, 1, :], -1.0, 0.0, op0=OP.mult, op1=OP.max)
            v.tensor_scalar(rtb[:, 1, :], rtb[:, 1, :], -1.0, 1.0, op0=OP.mult, op1=OP.add)
            v.tensor_tensor(rtb[:, 0, :], rtb[:, 0, :], rtb[:, 1, :], op=OP.mult)
            v.tensor_tensor(rwt[:].rearrange("p t e -> p t e"), rtb[:, 0, :].rearrange("p (g e) -> p g e", e=8),
                            bc(3), op=OP.mult)
            sm.tag(v.tensor_copy(rwb16[:].rearrange("p t e -> p (t e)"), rwt[:].rearrange("p t e -> p (t e)")), "v0rw")

        @blk.scalar
        def _(s):
            W(s, "v0r1", 1)
            sm.tag(s.activation(rtb[:, 1, :], rtb[:, 0, :], AF.Sign), "s0m1")
            W(s, "v0r2", 1)
            s.activation(rtb[:, 1, :], rtb[:, 1, :], AF.Sign)
            s.activation(rtb[:, 0, :], rtb[:, 0, :], AF.Exp)
            W(s, "v0r3", 1)
            sm.tag(s.activation(rtm[:, 2, :], rtm[:, 2, :], AF.Exp), "s0e")

        b_rwT = sm.val("v0rwT")
        b_t0rw = sm.val("t0rw")

        @blk.tensor
        def _(t):
            for tc in range(TC):
                W(t, "v0rw", 1)
                if tc >= 2:
                    W(t, "v0rwT", b_rwT + tc - 1)
                ps = psT if tc % 2 == 0 else psU
                sm.tag(t.transpose(ps[0:8, 0, :], rwb16[:, tc, :], ident[:]), "t0rw")

        @blk.vector
        def _(v):
            for tc in range(TC):
                W(v, "t0rw", b_t0rw + tc + 1)
                ps = psT if tc % 2 == 0 else psU
                sm.tag(v.tensor_copy(rwT[:, 128 * tc:128 * (tc + 1)], ps[0:8, 0, :]), "v0rwT")

        @blk.sync
        def _(s):
            W(s, "v0rwT", b_rwT + TC)
            dump(s, "rw", rwt[:].rearrange("p t e -> p (t e)"), [128, TC * 8])
            dump(s, "lgT", lgT[:], [8, N])
            dump(s, "m0", mvs[:].rearrange("p t x -> p (t x)"), [128, TC * 2])
            dump(s, "r0", r0s[:], [128, TC])
            dump(s, "xnTd", xnT[:].rearrange("p c n -> p (c n)"), [128, NF * N], BF16)

    sb.free("xt", "xn32", "xnb", "xnlo", "wrh", "wrl", "wrf", "lgT", "xnloT", "rwt", "rwb16")
    if upto <= 0:
        return fin(nc, sm, dbg_h)

    moe = sb.alloc("moe", [128, TC, D], F32)
    hpr = sb.alloc("hpr", [128, 2, 4, N], BF16)
    wgb = sb.alloc("wgb", [128, 2, NF, FEXP], BF16)
    wub = sb.alloc("wub", [128, 2, NF, FEXP], BF16)
    wdb = sb.alloc("wdb", [128, 2, 4, D], BF16)
    rwbb = sb.alloc("rwbb", [128, 1, N], BF16)
    sgl = sb.alloc("sgl", [128, 2, 512], BF16)
    up1 = sb.alloc("up1", [128, 2, 512], BF16)
    NQ = 16

    def qgate(gi):
        sl = gi % 3
        return psA[:, 1024 * sl:1024 * sl + 512] if sl < 2 else psD[:]

    def qup(gi):
        sl = gi % 3
        return psA[:, 1024 * sl + 512:1024 * sl + 1024] if sl < 2 else psE[:]

    def quarters_T(t, base, wg_l, wu_l, has_up=True, post=None, rwv=None):
        first2 = True
        for fk in range(4):
            for q in range(4):
                i = 4 * fk + q
                gi = base + i
                if gi % 3 == 2 and first2 and rwv is not None:
                    W(t, "v1rwb", rwv)
                    first2 = False
                if gi >= 3:
                    W(t, "s1sg", gi - 2)
                    if has_up:
                        W(t, "v1u", gi - 2)
                    else:
                        W(t, "v1h", gi - 2)
                for dc in range(NF):
                    insg = t.matmul(qgate(gi), wg_l(dc, fk),
                                    xnT[:, dc, 512 * q:512 * (q + 1)], start=(dc == 0), stop=(dc == NF - 1))
                    insu = t.matmul(qup(gi), wu_l(dc, fk),
                                    xnT[:, dc, 512 * q:512 * (q + 1)], start=(dc == 0), stop=(dc == NF - 1))
                sm.tag(insg, "t1g")
                sm.tag(insu, "t1u")
                if post is not None:
                    post(i)

    with nc.Block("p1", no_gpsimd_drain=True) as blk:
        @blk.gpsimd
        def _(g):
            W(g, "dO")
            for e in range(E):
                if e >= 2:
                    W(g, "t1u", NQ * (e - 1))
                if e == 0:
                    srcg = dWg[0].rearrange("(c p) f -> p c f", p=128)
                    srcu = dWu[0].rearrange("(c p) f -> p c f", p=128)
                    tagev(g.dma_start(wgb[:, 0, 0:4], srcg[:, 0:4]), "dgu0", ("g00",))
                    tagev(g.dma_start(wgb[:, 0, 4:8], srcg[:, 4:8]), "dgu1", ("g01",))
                    tagev(g.dma_start(wub[:, 0, 0:4], srcu[:, 0:4]), "dwd0", ("g02",))
                    tagev(g.dma_start(wub[:, 0, 4:8], srcu[:, 4:8]), "dwd1", ("g03",))
                else:
                    sm.tag(g.dma_start(wgb[:, e % 2], dWg[e].rearrange("(c p) f -> p c f", p=128)), "dgu%d" % (e % 2), 16)
                    tagev(g.dma_start(wub[:, e % 2], dWu[e].rearrange("(c p) f -> p c f", p=128)), "dgu%d" % (e % 2), ("gu", e))
                if e >= 2:
                    W(g, "t1wd", TC * ((e - 2) // 2 + 1))
                tagev(g.dma_start(wdb[:, e % 2], dWd[e].rearrange("(c p) f -> p c f", p=128)), "dwd%d" % (e % 2), ("wd", e))

        @blk.tensor
        def _(t):
            def dtc(elead, tc):
                if tc == 0:
                    Wev(t, ("wd", elead))
                    Wev(t, ("wd", elead + 1))
                    W(t, "v1h", NQ * (elead + 2))
                W(t, "v1acc", TC * (elead // 2) + tc)
                for j, ee in enumerate((elead, elead + 1)):
                    for fk in range(4):
                        for half in range(2):
                            ins = t.matmul(psB[:, 512 * half:512 * (half + 1)],
                                           hpr[:, ee % 2, fk, 128 * tc:128 * (tc + 1)],
                                           wdb[:, ee % 2, fk, 512 * half:512 * (half + 1)],
                                           start=(j == 0 and fk == 0), stop=(j == 1 and fk == 3))
                sm.tag(ins, "t1wd")

            for e in range(E):
                if e == 0:
                    for k2 in (("g00",), ("g01",), ("g02",), ("g03",)):
                        Wev(t, k2)
                else:
                    Wev(t, ("gu", e))
                if e >= 1:
                    last2 = max(g2 for g2 in range(NQ * e) if g2 % 3 == 2)
                    W(t, "s1sg", last2 + 1)
                    W(t, "v1u", last2 + 1)
                for hf in range(2):
                    W(t, "v1rwb", 2 * e + hf)
                    t.matmul(psD[:], ind8[:, 128 * e:128 * (e + 1)],
                             rwT[:, 1024 * hf:1024 * hf + 512], start=True, stop=True)
                    ins = t.matmul(psE[:], ind8[:, 128 * e:128 * (e + 1)],
                                   rwT[:, 1024 * hf + 512:1024 * (hf + 1)], start=True, stop=True)
                    sm.tag(ins, "t1rwb")
                quarters_T(t, NQ * e,
                           lambda dc, fk, e=e: wgb[:, e % 2, dc, 128 * fk:128 * (fk + 1)],
                           lambda dc, fk, e=e: wub[:, e % 2, dc, 128 * fk:128 * (fk + 1)],
                           rwv=2 * e + 2)
                if e % 2 == 1:
                    for tc in range(TC):
                        dtc(e - 1, tc)

        @blk.scalar
        def _(s):
            for e in range(E):
                for i in range(NQ):
                    gi = NQ * e + i
                    W(s, "t1g", gi + 1)
                    if gi >= 2:
                        W(s, "v1h", gi - 1)
                    sm.tag(s.activation(sgl[:, gi % 2, :], qgate(gi), AF.Silu), "s1sg")

        @blk.vector
        def _(v):
            def acc(elead, tc):
                W(v, "t1wd", TC * (elead // 2) + tc + 1)
                if elead // 2 == 0:
                    sm.tag(v.tensor_copy(moe[:, tc, :], psB[:]), "v1acc")
                else:
                    sm.tag(v.tensor_tensor(moe[:, tc, :], moe[:, tc, :], psB[:], op=OP.add), "v1acc")

            for e in range(E):
                for hf in range(2):
                    W(v, "t1rwb", 2 * e + hf + 1)
                    v.tensor_copy(rwbb[:, 0, 1024 * hf:1024 * hf + 512], psD[:])
                    sm.tag(v.tensor_copy(rwbb[:, 0, 1024 * hf + 512:1024 * (hf + 1)], psE[:]), "v1rwb")
                for i in range(NQ):
                    gi = NQ * e + i
                    fk, q = i // 4, i % 4
                    W(v, "t1u", gi + 1)
                    sm.tag(v.tensor_tensor(up1[:, gi % 2, :], qup(gi),
                                           rwbb[:, 0, 512 * q:512 * (q + 1)], op=OP.mult), "v1u")
                    W(v, "s1sg", gi + 1)
                    sm.tag(v.tensor_tensor(hpr[:, e % 2, fk, 512 * q:512 * (q + 1)],
                                           sgl[:, gi % 2, :], up1[:, gi % 2, :], op=OP.mult), "v1h")
                if e % 2 == 1:
                    for tc in range(TC):
                        acc(e - 1, tc)

    sb.free("wgb", "wub", "wdb", "hpr", "rwbb")
    sgw = sb.alloc("sgw", [128, 2, NF, 512], BF16)
    suw = sb.alloc("suw", [128, 2, NF, 512], BF16)
    sdw = sb.alloc("sdw", [128, 2, 4, D], BF16)
    hsh = sb.alloc("hsh", [128, 2, 4, N], BF16)
    gsp0 = sb.alloc("gsp0", [128, D], F32)
    gsp1 = sb.alloc("gsp1", [128, D], F32)

    with nc.Block("p1s", no_gpsimd_drain=True) as blk:
        @blk.gpsimd
        def _(g):
            for s2 in range(2):
                tagev(g.dma_start(sgw[:, s2], dSg[:, 512 * s2:512 * (s2 + 1)].rearrange("(c p) f -> p c f", p=128)), "dsw0", ("wsg", s2))
                tagev(g.dma_start(suw[:, s2], dSu[:, 512 * s2:512 * (s2 + 1)].rearrange("(c p) f -> p c f", p=128)), "dsw1", ("wsu", s2))
                tagev(g.dma_start(sdw[:, s2], dSd[512 * s2:512 * (s2 + 1), :].rearrange("(c p) f -> p c f", p=128)), "dsw2", ("wsd", s2))

        @blk.tensor
        def _(t):
            def sdtc(s2, tc):
                if tc == 0:
                    Wev(t, ("wsd", s2))
                    W(t, "v1h", NQ * E + NQ * (s2 + 1))
                W(t, "v1acc", TC * (4 + s2) + tc)
                for fk in range(4):
                    for half in range(2):
                        ins = t.matmul(psB[:, 512 * half:512 * (half + 1)],
                                       hsh[:, s2, fk, 128 * tc:128 * (tc + 1)],
                                       sdw[:, s2, fk, 512 * half:512 * (half + 1)],
                                       start=(fk == 0), stop=(fk == 3))
                sm.tag(ins, "t1sd")

            for s2 in range(2):
                Wev(t, ("wsg", s2))
                Wev(t, ("wsu", s2))
                post = (lambda i: sdtc(0, i)) if s2 == 1 else None
                quarters_T(t, NQ * E + NQ * s2,
                           lambda dc, fk, s2=s2: sgw[:, s2, dc, 128 * fk:128 * (fk + 1)],
                           lambda dc, fk, s2=s2: suw[:, s2, dc, 128 * fk:128 * (fk + 1)], has_up=False, post=post,
                           rwv=2 * E if s2 == 0 else None)
            for tc in range(TC):
                sdtc(1, tc)

        @blk.scalar
        def _(s):
            for s2 in range(2):
                for i in range(NQ):
                    gi = NQ * E + NQ * s2 + i
                    W(s, "t1g", gi + 1)
                    if gi >= 2:
                        W(s, "v1h", gi - 1)
                    sm.tag(s.activation(sgl[:, gi % 2, :], qgate(gi), AF.Silu), "s1sg")
            for tc in range(TC):
                W(s, "v1acc", TC * 5 + tc + 1)
                if tc >= 2:
                    s.wait_ge(sm.get("d1g%d" % ((tc - 2) % 2)), 16 * ((tc - 2) // 2 + 1))
                sm.tag(s.activation((gsp0 if tc % 2 == 0 else gsp1)[:], moe[:, tc, :], AF.Silu), "s1g")

        @blk.vector
        def _(v):
            for s2 in range(2):
                for i in range(NQ):
                    gi = NQ * E + NQ * s2 + i
                    fk, q = i // 4, i % 4
                    W(v, "t1u", gi + 1)
                    W(v, "s1sg", gi + 1)
                    sm.tag(v.tensor_tensor(hsh[:, s2, fk, 512 * q:512 * (q + 1)],
                                           sgl[:, gi % 2, :], qup(gi), op=OP.mult), "v1h")
                    if s2 == 1:
                        W(v, "t1sd", i + 1)
                        sm.tag(v.tensor_tensor(moe[:, i, :], moe[:, i, :], psB[:], op=OP.add), "v1acc")
            for tc in range(TC):
                W(v, "t1sd", TC + tc + 1)
                sm.tag(v.tensor_tensor(moe[:, tc, :], moe[:, tc, :], psB[:], op=OP.add), "v1acc")

        @blk.sync
        def _(s):
            for tc in range(TC):
                W(s, "s1g", tc + 1)
                tagev(s.dma_start(dgspill[128 * tc:128 * (tc + 1), :], (gsp0 if tc % 2 == 0 else gsp1)[:]), "d1g%d" % (tc % 2), ("g1", tc))
            Wev(s, ("g1", TC - 2))
            Wev(s, ("g1", TC - 1))
            dump(s, "moe", moe[:].rearrange("p t d -> p (t d)"), [128, TC * D])

    sb.free("sgw", "suw", "sdw", "hsh", "gsp0", "gsp1", "moe", "ind8", "rwT")
    if upto <= 1:
        return fin(nc, sm, dbg_h)

    sqq = sb.alloc("sqq", [128, TC, H], F32)
    sqk = sb.alloc("sqk", [128, TC, H], F32)
    nstq = sb.alloc("nstq", [128, TC, H], F32)
    nstk = sb.alloc("nstk", [128, TC, H], F32)
    qT = sb.alloc("qT", [128, NF, N], BF16)
    kT = sb.alloc("kT", [128, NF, N], BF16)
    vaug = sb.alloc("vaug", [128, TC, H * 65], BF16)
    wsw = sb.alloc("wsw", [128, 2, NF, D], BF16)
    qkb = sb.alloc("qkb", [128, 2, D], BF16)
    sqscr = sb.alloc("sqscr", [128, 2, D], F32)

    sweeps = (("q", dwq, qT, sqq), ("k", dwk, kT, sqk), ("v", dwv, None, None))

    with nc.Block("p2", no_gpsimd_drain=True) as blk:
        @blk.gpsimd
        def _(g):
            W(g, "dO")
            Wev(g, ("g1", TC - 2))
            Wev(g, ("g1", TC - 1))
            for si, (nm, dw, tT, sq) in enumerate(sweeps):
                if si >= 2:
                    W(g, "t2mm", TC * (si - 1))
                tagev(g.dma_start(wsw[:, si % 2], dw.rearrange("(c p) f -> p c f", p=128)), "dsw%d" % si, ("sw", si))

        @blk.vector
        def _(v):
            sm.tag(v.memset(vaug[:].rearrange("p t w -> p (t w)"), 1.0), "v2i")

        @blk.tensor
        def _(t):
            def tr2(g):
                W(t, "v2qk", g + 1)
                if g >= 2:
                    W(t, "s2ev", g - 1)
                pT = psT if g % 2 == 0 else psU
                for c in range(NF):
                    ins = t.transpose(pT[:, c, 0:128], qkb[:, g % 2, 128 * c:128 * (c + 1)], ident[:])
                sm.tag(ins, "t2tr")

            W(t, "v2i")
            for si, (nm, dw, tT, sq) in enumerate(sweeps):
                Wev(t, ("sw", si))
                for tc in range(TC):
                    gi = TC * si + tc
                    if gi >= 2:
                        if gi - 2 < 2 * TC:
                            W(t, "v2qk", gi - 1)
                            W(t, "s2sq", gi - 1)
                        else:
                            W(t, "v2ev", gi - 1 - 2 * TC)
                    for dc in range(NF):
                        for half in range(2):
                            ins = t.matmul(psA[:, 1024 * (gi % 2) + 512 * half:1024 * (gi % 2) + 512 * (half + 1)],
                                           xnT[:, dc, 128 * tc:128 * (tc + 1)],
                                           wsw[:, si % 2, dc, 512 * half:512 * (half + 1)], start=(dc == 0), stop=(dc == NF - 1))
                    sm.tag(ins, "t2mm")
                    if 1 <= gi <= 2 * TC:
                        tr2(gi - 1)

        @blk.scalar
        def _(s):
            def ev2(g):
                W(s, "t2tr", g + 1)
                tTp = qT if g < TC else kT
                tcp = g % TC
                pT = psT if g % 2 == 0 else psU
                sm.tag(s.copy(tTp[:, 0:NF, 128 * tcp:128 * (tcp + 1)], pT[:, 0:NF, 0:128]), "s2ev")

            for si in range(2):
                for tc in range(TC):
                    gi = TC * si + tc
                    W(s, "t2mm", gi + 1)
                    if gi >= 2:
                        W(s, "v2qk", gi - 1)
                    sm.tag(s.activation(sqscr[:, gi % 2, :], psA[:, 1024 * (gi % 2):1024 * (gi % 2) + 1024], AF.Square), "s2sq")
                    if gi >= 1:
                        ev2(gi - 1)
            ev2(2 * TC - 1)

        @blk.vector
        def _(v):
            for si, (nm, dw, tT, sq) in enumerate(sweeps):
                for tc in range(TC):
                    gi = TC * si + tc
                    W(v, "t2mm", gi + 1)
                    src = psA[:, 1024 * (gi % 2):1024 * (gi % 2) + 1024]
                    if si < 2:
                        W(v, "s2sq", gi + 1)
                        sm.tag(v.tensor_reduce(sq[:, tc, :], sqscr[:, gi % 2, :].rearrange("p (h w) -> p h w", w=KH),
                                               axis=AX.X, op=OP.add), "v2sq")
                        if gi >= 2:
                            W(v, "t2tr", gi - 1)
                        sm.tag(v.tensor_copy(qkb[:, gi % 2, :], src), "v2qk")
                    else:
                        sm.tag(v.tensor_copy(vaug[:, tc, :].rearrange("p (h w) -> p h w", w=65)[:, :, 0:64],
                                             src.rearrange("p (h w) -> p h w", w=KH)), "v2ev")

        @blk.sync
        def _(s):
            W(s, "s2ev", 2 * TC)
            W(s, "v2qk", 2 * TC)
            W(s, "v2ev", TC)
            dump(s, "qTd", qT[:].rearrange("p c n -> p (c n)"), [128, NF * N], BF16)
            dump(s, "kTd", kT[:].rearrange("p c n -> p (c n)"), [128, NF * N], BF16)
            dump(s, "vaugd", vaug[:].rearrange("p t w -> p (t w)"), [128, TC * H * 65], BF16)
            dump(s, "sqqd", sqq[:].rearrange("p t h -> p (t h)"), [128, TC * H])
            dump(s, "sqkd", sqk[:].rearrange("p t h -> p (t h)"), [128, TC * H])

    sb.free("wsw", "qkb", "sqscr", "xnT")
    if upto <= 2:
        return fin(nc, sm, dbg_h)

    omT = sb.alloc("omTs", [KH, M], BF16)
    omTD = sb.alloc("omTD", [128, M], BF16)
    ombd = sb.alloc("ombd", [128, 512], BF16)
    cqT = sb.alloc("cqT", [32, N], BF16)
    ckT = sb.alloc("ckT", [32, N], BF16)
    indq = sb.alloc("indq", [32, 2048], BF16)
    indk = sb.alloc("indk", [32, 4096], BF16)

    with nc.Block("p3a", no_gpsimd_drain=True) as blk:
        @blk.gpsimd
        def _(g):
            W(g, "dO")
            tagev(g.dma_start(omT[:], domT), "dWg", ("om",))
            g.iota(indq[:], pattern=[[2, H], [0, 128]], channel_multiplier=0, allow_small_or_imprecise_dtypes=True)
            sm.tag(g.iota(indk[:], pattern=[[2, H], [0, 256]], channel_multiplier=0, allow_small_or_imprecise_dtypes=True), "g3io")

        @blk.vector
        def _(v):
            W(v, "g3io")
            Wev(v, ("om",))
            v.tensor_scalar(omT[:], omT[:], SCALE, None, op0=OP.mult)
            v.memset(ombd[:], 0.0)
            v.tensor_copy(ombd[0:64, 0:256], omT[:])
            v.tensor_copy(omTD[0:64, :], omT[:])
            v.tensor_copy(omTD[64:128, :], omT[:])
            sm.tag(v.tensor_copy(ombd[64:128, 256:512], omT[:]), "v3om")
            v.tensor_scalar(rv[0:32, 0:1], io32[:], -1.0, 0.5, op0=OP.mult, op1=OP.add)
            sm.tag(v.memset(rv[0:32, 9:10], -1.0), "v3ic")

        @blk.scalar
        def _(s):
            W(s, "v3ic")
            sm.tag(s.activation(indq[:], indq[:], AF.Abs, bias=rv[0:32, 0:1], scale=1.0), "sch")
            W(s, "sch")
            sm.tag(s.activation(indq[:], indq[:], AF.Sign, bias=rv[0:32, 9:10], scale=1.0), "sch")
            W(s, "sch")
            sm.tag(s.activation(indk[:], indk[:], AF.Abs, bias=rv[0:32, 0:1], scale=1.0), "sch")
            W(s, "sch")
            sm.tag(s.activation(indk[:], indk[:], AF.Sign, bias=rv[0:32, 9:10], scale=1.0), "s3ic")

        @blk.vector
        def _(v):
            W(v, "s3ic")
            v.tensor_scalar(indq[:], indq[:], -0.5, 0.5, op0=OP.mult, op1=OP.add)
            sm.tag(v.tensor_scalar(indk[:], indk[:], -0.5, 0.5, op0=OP.mult, op1=OP.add), "v3ic")

        @blk.tensor
        def _(t):
            W(t, "v3om")
            for si, tT in enumerate((kT, qT)):
                for tc in range(TC):
                    for g2 in range(4):
                        gi = 64 * si + 4 * tc + g2
                        if gi >= 2:
                            W(t, "v3st", gi - 1)
                        for j in range(2):
                            fc = 2 * g2 + j
                            ins = t.matmul(psA[:, 1024 * (gi % 2) + 512 * j:1024 * (gi % 2) + 512 * (j + 1)],
                                           tT[:, fc, 128 * tc:128 * (tc + 1)], ombd[:], start=True, stop=True)
                        sm.tag(ins, "t3st")

        @blk.vector
        def _(v):
            for si, nst in enumerate((nstk, nstq)):
                for tc in range(TC):
                    for g2 in range(4):
                        gi = 64 * si + 4 * tc + g2
                        W(v, "t3st", gi + 1)
                        sm.tag(v.tensor_reduce(nst[:, tc, 4 * g2:4 * (g2 + 1)],
                                               psA[:, 1024 * (gi % 2):1024 * (gi % 2) + 1024].rearrange("p (h w) -> p h w", w=M),
                                               axis=AX.X, op=OP.max, negate=True), "v3st")

        @blk.vector
        def _(v):
            W(v, "v3st", 128)
            W(v, "v2sq", 32)
            for si, (sq, nst, cT) in enumerate(((sqk, nstk, ckT), (sqq, nstq, cqT))):
                sqf = sq[:].rearrange("p t h -> p (t h)")
                nstf = nst[:].rearrange("p t h -> p (t h)")
                v.tensor_scalar(cbf[:], sqf, SQC, LNRTM, op0=OP.mult, op1=OP.add)
                sm.tag(v.tensor_tensor(cbf[:], nstf, cbf[:], op=OP.subtract), "vch")
                W(v, "vch")
                ev3 = cb32[:].rearrange("p t (h two) -> p (t h) two", two=2)
                sm.tag(v.tensor_copy(ev3[:, :, 0:1].rearrange("p a one -> p (a one)"), cbf[:]), "vch")
                W(v, "vch")
                sm.tag(v.tensor_tensor(ev3[:, :, 1:2].rearrange("p a one -> p (a one)"), cbf[:],
                                       ev3[:, :, 0:1].rearrange("p a one -> p (a one)"), op=OP.subtract), "v3c")
                for tc in range(TC):
                    W(v, "t3c", 16 * si + tc + 1)
                    ps = psT if (16 * si + tc) % 2 == 0 else psU
                    sm.tag(v.tensor_copy(cT[:, 128 * tc:128 * (tc + 1)], ps[0:32, 0, :]), "v3cT")

        @blk.tensor
        def _(t):
            for si in range(2):
                W(t, "v3c", si + 1)
                for tc in range(TC):
                    if 16 * si + tc >= 2:
                        W(t, "v3cT", 16 * si + tc - 1)
                    ps = psT if (16 * si + tc) % 2 == 0 else psU
                    sm.tag(t.transpose(ps[0:32, 0, :], cb32[:, tc, :], ident[:]), "t3c")

        @blk.sync
        def _(s):
            W(s, "v3cT", 32)
            dump(s, "nstkd", nstk[:].rearrange("p t h -> p (t h)"), [128, TC * H])
            dump(s, "nstqd", nstq[:].rearrange("p t h -> p (t h)"), [128, TC * H])
            dump(s, "ckTd", ckT[:], [32, N], BF16)
            dump(s, "cqTd", cqT[:], [32, N], BF16)

    sb.free("sqq", "sqk", "nstq", "nstk")
    if upto <= 3:
        return fin(nc, sm, dbg_h)

    pk = sb.alloc("pk", [128, TC, 512], BF16)
    pse = sb.alloc("pse", [128, 2, 512], F32)
    kvzTs = sb.alloc("kvzTs", [65, 512], BF16)
    kvz = sb.alloc("kvz", [128, 32, 65], BF16)

    with nc.Block("p3b", no_gpsimd_drain=True) as blk:
        @blk.tensor
        def _(t):
            for fc in range(NF):
                for tc in range(TC):
                    gi = TC * fc + tc
                    if gi >= 4:
                        W(t, "s3e", gi - 3)
                    sl = gi % 4
                    t.matmul(psA[:, 512 * sl:512 * (sl + 1)], kT[:, fc, 128 * tc:128 * (tc + 1)],
                             ombd[:], start=True, stop=False)
                    sm.tag(t.matmul(psA[:, 512 * sl:512 * (sl + 1)], ckT[:, 128 * tc:128 * (tc + 1)],
                                    indk[:, 512 * fc:512 * (fc + 1)], start=False, stop=True), "t3p")
                    if tc >= 1:
                        W(t, "v3pk", TC * fc + tc)
                        for hh in range(2):
                            ins = t.matmul(psB[0:65, 512 * hh:512 * hh + 256],
                                           vaug[:, tc - 1, :].rearrange("p (h w) -> p h w", w=65)[:, 2 * fc + hh, :],
                                           pk[:, tc - 1, 256 * hh:256 * (hh + 1)],
                                           start=(tc - 1 == 0), stop=(tc - 1 == TC - 1))
                        sm.tag(ins, "t3kv")
                W(t, "v3pk", TC * (fc + 1))
                for hh in range(2):
                    ins = t.matmul(psB[0:65, 512 * hh:512 * hh + 256],
                                   vaug[:, TC - 1, :].rearrange("p (h w) -> p h w", w=65)[:, 2 * fc + hh, :],
                                   pk[:, TC - 1, 256 * hh:256 * (hh + 1)], start=False, stop=True)
                sm.tag(ins, "t3kv")
                W(t, "s3kv", fc + 1)
                if fc >= 1:
                    W(t, "v3kz", fc)
                for j in range(4):
                    ins = t.transpose(psU[:, j, 0:65], kvzTs[0:65, 128 * j:128 * (j + 1)], ident[0:65, 0:65])
                sm.tag(ins, "t3kT")

        @blk.scalar
        def _(s):
            for fc in range(NF):
                for tc in range(TC):
                    gi = TC * fc + tc
                    W(s, "t3p", gi + 1)
                    if gi >= 2:
                        W(s, "v3pk", gi - 1)
                    sl = gi % 4
                    sm.tag(s.activation(pse[:, gi % 2, :], psA[:, 512 * sl:512 * (sl + 1)], AF.Exp), "s3e")
                W(s, "t3kv", TC * (fc + 1))
                sm.tag(s.copy(kvzTs[:].rearrange("p (b c) -> p b c", b=2),
                              psB[0:65, 0:1024].rearrange("p (b c) -> p b c", b=2)[:, :, 0:256]), "s3kv")

        @blk.vector
        def _(v):
            for fc in range(NF):
                for tc in range(TC):
                    gi = TC * fc + tc
                    W(v, "s3e", gi + 1)
                    sm.tag(v.tensor_scalar(pk[:, tc, :], pse[:, gi % 2, :], 1e-6, None, op0=OP.add), "v3pk")
                W(v, "t3kT", fc + 1)
                sm.tag(v.tensor_copy(kvz[:, 4 * fc:4 * (fc + 1), :], psU[:, 0:4, 0:65]), "v3kz")

        @blk.sync
        def _(s):
            W(s, "v3kz", NF)
            dump(s, "kvzd", kvz[:].rearrange("p a b -> p (a b)"), [128, 32 * 65], BF16)

    sb.free("pk", "vaug", "ombd", "indk", "ckT", "kT", "sgl", "up1", "pse", "kvzTs")
    if upto <= 4:
        return fin(nc, sm, dbg_h)

    attT = sb.alloc("attT", [128, NF, N], BF16)
    pqT = sb.alloc("pqT", [128, 2, 2, N], BF16)
    numTs = sb.alloc("numTs", [64, 2, N], F32)
    denb = sb.alloc("denb", [65, 2, N], BF16)
    lnv = sb.alloc("lnv", [65, 2, 1024], F32)
    pse4 = sb.alloc("pse4", [128, 4, N], BF16)
    cone = sb.alloc("cone", [128, 64], BF16)
    wob = sb.alloc("wob", [128, NF, D], BF16)

    with nc.Block("p4b", no_gpsimd_drain=True) as blk:
        @blk.gpsimd
        def _(g):
            tagev(g.dma_start(wob[:], dwo.rearrange("(c p) f -> p c f", p=128)), "dW4", ("wo",))

        @blk.tensor
        def _(t):
            W(t, "v4a", 1)

            def ARG(h):
                fc, hh = h // 2, h % 2
                for mc in range(2):
                    gi = 2 * h + mc
                    for j in range(2):
                        if gi >= 1:
                            W(t, "s4e", 2 * (gi - 1) + j + 1)
                        for pc in (2 * j, 2 * j + 1):
                            t.matmul(psA[:, 512 * pc:512 * (pc + 1)],
                                     omTD[64 * hh:64 * hh + 64, 128 * mc:128 * (mc + 1)],
                                     qT[64 * hh:64 * hh + 64, fc, 512 * pc:512 * (pc + 1)], start=True, stop=False)
                        for pc in (2 * j, 2 * j + 1):
                            ins = t.matmul(psA[:, 512 * pc:512 * (pc + 1)], indq[:, 128 * h:128 * (h + 1)],
                                           cqT[:, 512 * pc:512 * (pc + 1)], start=False, stop=True)
                        sm.tag(ins, "t4p")

            def NUMB(h):
                for half in range(2):
                    if half == 0:
                        W(t, "v4pq", 2 * h + 2)
                        if h >= 1:
                            W(t, "v4n", 2 * h)
                            W(t, "s4n", 2 * h)
                    else:
                        W(t, "v4n", 2 * h + 1)
                        W(t, "s4n", 2 * h + 1)
                    for pc in range(2):
                        for mc in range(2):
                            ins = t.matmul(psB[0:65, 512 * pc:512 * (pc + 1)], kvz[:, 2 * h + mc, :],
                                           pqT[:, h % 2, mc, 1024 * half + 512 * pc:1024 * half + 512 * (pc + 1)],
                                           start=(mc == 0), stop=(mc == 1))
                    sm.tag(ins, "t4n")
                for half in range(2):
                    W(t, "s4n", 2 * h + half + 1)
                    if h >= 1 or half == 1:
                        W(t, "v4a", 2 * h + half + 1)
                    t.matmul(psD[0:64, :], cone[64:65, 0:64],
                             denb[64:65, h % 2, 1024 * half:1024 * half + 512], start=True, stop=True)
                    ins = t.matmul(psE[0:64, :], cone[64:65, 0:64],
                                   denb[64:65, h % 2, 1024 * half + 512:1024 * (half + 1)], start=True, stop=True)
                    sm.tag(ins, "t4b")

            for step in range(H + 1):
                if step < H:
                    ARG(step)
                if step >= 1:
                    NUMB(step - 1)

        @blk.scalar
        def _(s):
            def den(h, half):
                if half == 0 and h >= 2:
                    W(s, "t4b", 2 * h - 2)
                W(s, "t4n", 2 * h + half + 1)
                sm.tag(s.activation(lnv[64:65, 0, :], psB[64:65, 0:1024], AF.Ln), "sch")
                W(s, "sch")
                sm.tag(s.activation(denb[64:65, h % 2, 1024 * half:1024 * (half + 1)],
                                    lnv[64:65, 0, :], AF.Exp, scale=-1.0), "s4n")

            for step in range(H + 1):
                if step < H:
                    h = step
                    for mc in range(2):
                        gi = 2 * h + mc
                        for j in range(2):
                            W(s, "t4p", 2 * gi + j + 1)
                            if gi >= 4 and j == 0:
                                W(s, "v4pq", gi - 3)
                            sm.tag(s.activation(pse4[:, gi % 4, 1024 * j:1024 * (j + 1)],
                                                psA[:, 1024 * j:1024 * (j + 1)], AF.Exp), "s4e")
                if step >= 1:
                    for half in range(2):
                        den(step - 1, half)

        @blk.vector
        def _(v):
            sm.tag(v.memset(cone[:], 1.0), "v4a")
            for step in range(H + 1):
                if step < H:
                    h = step
                    if h >= 2:
                        W(v, "t4n", 2 * (h - 1))
                    for mc in range(2):
                        gi = 2 * h + mc
                        W(v, "s4e", 2 * gi + 2)
                        sm.tag(v.tensor_scalar(pqT[:, h % 2, mc, :], pse4[:, gi % 4, :], 1e-6, None, op0=OP.add), "v4pq")
                if step >= 1:
                    h = step - 1
                    fc, hh = h // 2, h % 2
                    for half in range(2):
                        W(v, "t4n", 2 * h + half + 1)
                        sm.tag(v.tensor_copy(numTs[0:64, h % 2, 1024 * half:1024 * (half + 1)], psB[0:64, :]), "v4n")
                        W(v, "t4b", 2 * h + half + 1)
                        v.tensor_tensor(attT[64 * hh:64 * hh + 64, fc, 1024 * half:1024 * half + 512],
                                        numTs[0:64, h % 2, 1024 * half:1024 * half + 512],
                                        psD[0:64, :], op=OP.mult)
                        sm.tag(v.tensor_tensor(attT[64 * hh:64 * hh + 64, fc, 1024 * half + 512:1024 * (half + 1)],
                                               numTs[0:64, h % 2, 1024 * half + 512:1024 * (half + 1)],
                                               psE[0:64, :], op=OP.mult), "v4a")

        @blk.sync
        def _(s):
            W(s, "v4a", 2 * H + 1)
            dump(s, "attTd", attT[:].rearrange("p c n -> p (c n)"), [128, NF * N], BF16)

    sb.free("pqT", "numTs", "denb", "lnv", "pse4", "cone", "kvz", "qT", "omTs", "omTD", "indq", "cqT")
    if upto <= 5:
        return fin(nc, sm, dbg_h)

    vsml = sb.alloc("vsml", [1, D], F32)
    ln1gb = sb.alloc("ln1gb", [128, D], F32)
    ln1bb = sb.alloc("ln1bb", [128, D], F32)
    ln2gb = sb.alloc("ln2gb", [128, D], F32)
    ln2bb = sb.alloc("ln2bb", [128, D], F32)
    b2b = sb.alloc("b2b", [128, D], F32)

    xt2 = sb.alloc("xt2", [128, 3, D], F32)
    scr1 = sb.alloc("scr1", [128, D], F32)
    scr2 = sb.alloc("scr2", [128, D], F32)
    x1t = sb.alloc("x1t", [128, 2, D], F32)
    x1bt = sb.alloc("x1bt", [128, 2, D], BF16)
    x1Ts = sb.alloc("x1Ts", [128, 2, NF, 128], BF16)
    w1bs = [sb.alloc("w1b%d" % i, [128, 2, FF], BF16) for i in range(4)]

    with nc.Block("p5", no_gpsimd_drain=True) as blk:
        @blk.sync
        def _(s):
            for i, (dv, tgt) in enumerate(((dln1g, ln1gb), (dln1b, ln1bb), (dln2g, ln2gb), (dln2b, ln2bb), (db2, b2b))):
                if i >= 1:
                    W(s, "t5v", i)
                tagev(s.dma_start(vsml[:], dv[None, :]), "d5v", ("v5", i))

        @blk.tensor
        def _(t):
            for i in range(5):
                Wev(t, ("v5", i))
                if i >= 1:
                    W(t, "v5v", i)
                for sl in range(2):
                    ins = t.matmul(psB[:, 512 * sl:512 * (sl + 1)], onesf[:], vsml[0:1, 512 * sl:512 * (sl + 1)],
                                   start=True, stop=True)
                sm.tag(ins, "t5v")

        @blk.vector
        def _(v):
            for i, tgt in enumerate((ln1gb, ln1bb, ln2gb, ln2bb, b2b)):
                W(v, "t5v", i + 1)
                sm.tag(v.tensor_copy(tgt[:], psB[:]), "v5v")

        @blk.gpsimd
        def _(g):
            W(g, "dO")
            for dc in range(NF):
                tagev(g.dma_start(w1bs[dc // 2][:, dc % 2, :], dw1[128 * dc:128 * (dc + 1), :]), "dW1", ("w1",))

        @blk.sync
        def _(s):
            def spill(tc):
                W(s, "v5x1", 2 * (tc + 1))
                tagev(s.dma_start(dx1spill[128 * tc:128 * (tc + 1), :], x1t[:, tc % 2, :]), "dsp%d" % (tc % 2), ("sp", tc))
                W(s, "s5ev", tc + 1)
                tagev(s.dma_start(dx1T[:, :, 128 * tc:128 * (tc + 1)].rearrange("c p f -> p c f"),
                                  x1Ts[:, tc % 2]), "dspT%d" % (tc % 2), ("spT", tc))
            for tc in range(TC):
                if tc >= 3:
                    W(s, "v5pre", tc - 2)
                tagev(s.dma_start(xt2[:, tc % 3, :], dx[128 * tc:128 * (tc + 1), :]), "d5x%d" % (tc % 3), ("x5", tc))
                if tc >= 2:
                    spill(tc - 2)
            spill(TC - 2)
            spill(TC - 1)
            for key in (("sp", TC - 2), ("sp", TC - 1), ("spT", TC - 2), ("spT", TC - 1)):
                Wev(s, key)

        @blk.tensor
        def _(t):
            def tr5(g):
                W(t, "v5x1", 2 * (g + 1))
                if g >= 2:
                    W(t, "s5ev", g - 1)
                pT = psT if g % 2 == 0 else psU
                for c in range(NF):
                    ins = t.transpose(pT[:, c, 0:128], x1bt[:, g % 2, 128 * c:128 * (c + 1)], ident[:])
                sm.tag(ins, "t5tr")

            Wev(t, ("wo",))
            for tc in range(TC):
                if tc >= 2:
                    W(t, "v5pre", tc - 1)
                for fc in range(NF):
                    for half in range(2):
                        ins = t.matmul(psA[:, 1024 * (tc % 2) + 512 * half:1024 * (tc % 2) + 512 * (half + 1)],
                                       attT[:, fc, 128 * tc:128 * (tc + 1)],
                                       wob[:, fc, 512 * half:512 * (half + 1)], start=(fc == 0), stop=(fc == NF - 1))
                sm.tag(ins, "t5wo")
                if tc >= 1:
                    tr5(tc - 1)
            tr5(TC - 1)

        @blk.vector
        def _(v):
            for tc in range(TC):
                W(v, "t5wo", tc + 1)
                Wev(v, ("x5", tc))
                if tc >= 2:
                    W(v, "t5tr", tc - 1)
                v.tensor_scalar(scr1[:], xt2[:, tc % 3, :], mvs[:, tc, 0:1], r0s[:, tc:tc + 1],
                                op0=OP.subtract, op1=OP.mult)
                sm.tag(v.tensor_tensor(scr2[:], psA[:, 1024 * (tc % 2):1024 * (tc % 2) + 1024],
                                       scr1[:], op=OP.add), "v5pre")
                v.bn_stats(bns[:, 0, :], scr2[:, 0:512])
                sm.tag(v.bn_stats(bns[:, 1, :], scr2[:, 512:1024]), "vch")
                W(v, "vch")
                sm.tag(v.bn_aggr(mvx[:], bns[:].rearrange("p a b -> p (a b)")), "v5var")
                W(v, "s5r", tc + 1)
                v.scalar_tensor_tensor(scr1[:], scr2[:], mvx[:, 0:1], ln1gb[:],
                                       op0=OP.subtract, op1=OP.mult)
                v.tensor_scalar(scr1[:], scr1[:], rv[:, 1:2], None, op0=OP.mult)
                if tc >= 2:
                    Wev(v, ("sp", tc - 2))
                sm.tag(v.tensor_tensor(x1t[:, tc % 2, :], scr1[:], ln1bb[:], op=OP.add), "v5x1")
                sm.tag(v.tensor_copy(x1bt[:, tc % 2, :], x1t[:, tc % 2, :]), "v5x1")

        @blk.scalar
        def _(s):
            for tc in range(TC):
                W(s, "v5var", tc + 1)
                sm.tag(s.activation(rv[:, 4:5], mvx[:, 1:2], AF.Ln, bias=epsv1[:]), "sch")
                W(s, "sch")
                sm.tag(s.activation(rv[:, 1:2], rv[:, 4:5], AF.Exp, scale=-0.5), "s5r")
                W(s, "t5tr", tc + 1)
                if tc >= 2:
                    Wev(s, ("spT", tc - 2))
                pT = psT if tc % 2 == 0 else psU
                sm.tag(s.copy(x1Ts[:, tc % 2], pT[:, 0:NF, 0:128]), "s5ev")

    sb.free("attT", "wob", "xt2", "scr1", "x1bt", "x1t", "x1Ts", "ln1gb", "ln1bb", "vsml")
    if upto <= 6:
        return fin(nc, sm, dbg_h)

    w2bs = [sb.alloc("w2b%d" % i, [128, 8, D], BF16) for i in range(4)]
    x1Tp = sb.alloc("x1Tp", [128, 2, NF, 256], BF16)
    h1p = sb.alloc("h1p", [128, 32, 256], BF16)
    x1r = sb.alloc("x1r", [128, 2, D], F32)
    gr = sb.alloc("gr", [128, 2, D], F32)
    xr0 = sb.alloc("xr0", [128, D], F32)
    xr1 = sb.alloc("xr1", [128, D], F32)

    with nc.Block("p6", no_gpsimd_drain=True) as blk:
        @blk.gpsimd
        def _(g):
            W(g, "dO")
            for key in (("sp", TC - 2), ("sp", TC - 1), ("spT", TC - 2), ("spT", TC - 1)):
                Wev(g, key)
            for i in range(4):
                tagev(g.dma_start(w2bs[i][:],
                                  dw2[1024 * i:1024 * (i + 1), :].rearrange("(c p) f -> p c f", p=128)), "dW2b", ("w2b",))

        @blk.sync
        def _(s):
            for key in (("g1", TC - 2), ("g1", TC - 1), ("sp", TC - 2), ("sp", TC - 1),
                        ("spT", TC - 2), ("spT", TC - 1)):
                Wev(s, key)
            for p in range(8):
                if p >= 2:
                    W(s, "t6w1", 32 * (p - 1))
                tagev(s.dma_start(x1Tp[:, p % 2],
                                  dx1T[:, :, 256 * p:256 * (p + 1)].rearrange("c p f -> p c f")), "d6xT%d" % (p % 2), ("xT6", p))
                for tcl in range(2):
                    tc = 2 * p + tcl
                    if tc >= 2:
                        W(s, "v6fin", tc - 1)
                        Wev(s, ("oo", tc - 2))
                    tagev(s.dma_start(x1r[:, tc % 2, :], dx1spill[128 * tc:128 * (tc + 1), :]), "d6a%d" % (tc % 2), ("r6a", tc))
                    tagev(s.dma_start(gr[:, tc % 2, :], dgspill[128 * tc:128 * (tc + 1), :]), "d6b%d" % (tc % 2), ("r6b", tc))
                    tagev(s.dma_start((xr0 if tc % 2 == 0 else xr1)[:], dx[128 * tc:128 * (tc + 1), :]), "d6c%d" % (tc % 2), ("r6c", tc))
                    if tc >= 1:
                        W(s, "v6fin", tc)
                        tagev(s.dma_start(dout[128 * (tc - 1):128 * tc, :], x1r[:, (tc - 1) % 2, :]), "dOo%d" % ((tc - 1) % 2), ("oo", tc - 1))
            W(s, "v6fin", TC)
            tagev(s.dma_start(dout[128 * (TC - 1):128 * TC, :], x1r[:, (TC - 1) % 2, :]), "dOo%d" % ((TC - 1) % 2), ("oo", TC - 1))

        @blk.tensor
        def _(t):
            Wev(t, ("w1",))
            for p in range(8):
                Wev(t, ("xT6", p))
                for fk in range(32):
                    gi = 32 * p + fk
                    if gi >= 4:
                        W(t, "s6g", gi - 3)
                    sl = fk % 4
                    for dc in range(NF):
                        ins = t.matmul(psA[:, 512 * sl:512 * sl + 256],
                                       w1bs[dc // 2][:, dc % 2, 128 * fk:128 * (fk + 1)],
                                       x1Tp[:, p % 2, dc, :], start=(dc == 0), stop=(dc == NF - 1))
                    sm.tag(ins, "t6w1")
                if p == 0:
                    Wev(t, ("w2b",))
                for tcl in range(2):
                    tc = 2 * p + tcl
                    if tc >= 1:
                        W(t, "v6pre", tc)
                    for fk in range(32):
                        if tcl == 0:
                            W(t, "s6g", 32 * p + fk + 1)
                        for half in range(2):
                            ins = t.matmul(psB[:, 512 * half:512 * (half + 1)],
                                           h1p[:, fk, 128 * tcl:128 * (tcl + 1)],
                                           w2bs[fk // 8][:, fk % 8, 512 * half:512 * (half + 1)], start=(fk == 0), stop=(fk == 31))
                    sm.tag(ins, "t6w2")

        @blk.scalar
        def _(s):
            for p in range(8):
                for fk in range(32):
                    gi = 32 * p + fk
                    W(s, "t6w1", gi + 1)
                    sl = fk % 4
                    sm.tag(s.activation(h1p[:, fk, :], psA[:, 512 * sl:512 * sl + 256], AF.Gelu,
                                        bias=b1s[:, fk:fk + 1]), "s6g")
                for tcl in range(2):
                    tc = 2 * p + tcl
                    W(s, "v6var", tc + 1)
                    sm.tag(s.activation(rv[:, 4:5], mvx[:, 1:2], AF.Ln, bias=epsv1[:]), "sch")
                    W(s, "sch")
                    sm.tag(s.activation(rv[:, 1:2], rv[:, 4:5], AF.Exp, scale=-0.5), "s6r")

        @blk.vector
        def _(v):
            for p in range(8):
                for tcl in range(2):
                    tc = 2 * p + tcl
                    W(v, "t6w2", tc + 1)
                    Wev(v, ("r6a", tc))
                    Wev(v, ("r6b", tc))
                    Wev(v, ("r6c", tc))
                    if tc >= 2:
                        Wev(v, ("oo", tc - 2))
                    v.tensor_tensor(scr2[:], psB[:], b2b[:], op=OP.add)
                    sm.tag(v.tensor_tensor(scr2[:], scr2[:], x1r[:, tc % 2, :], op=OP.add), "v6pre")
                    v.bn_stats(bns[:, 0, :], scr2[:, 0:512])
                    sm.tag(v.bn_stats(bns[:, 1, :], scr2[:, 512:1024]), "vch")
                    W(v, "vch")
                    sm.tag(v.bn_aggr(mvx[:], bns[:].rearrange("p a b -> p (a b)")), "v6var")
                    W(v, "s6r", tc + 1)
                    v.scalar_tensor_tensor(sqv[:], scr2[:], mvx[:, 0:1], ln2gb[:],
                                           op0=OP.subtract, op1=OP.mult)
                    v.tensor_scalar(sqv[:], sqv[:], rv[:, 1:2], None, op0=OP.mult)
                    v.tensor_tensor(sqv[:], sqv[:], ln2bb[:], op=OP.add)
                    if "rawffn" in debug:
                        sm.tag(v.tensor_copy(x1r[:, tc % 2, :], scr2[:]), "v6fin")
                    else:
                        v.tensor_tensor(sqv[:], sqv[:], gr[:, tc % 2, :], op=OP.mult)
                        sm.tag(v.tensor_tensor(x1r[:, tc % 2, :], sqv[:], (xr0 if tc % 2 == 0 else xr1)[:], op=OP.add), "v6fin")

    return fin(nc, sm, dbg_h)


def fin(nc, sm, dbg_h):
    with nc.Block("fin", no_gpsimd_drain=True) as blk:
        @blk.sync
        def _(s):
            for name in sorted(sm.v):
                if name.startswith("d") and sm.v[name] > 0:
                    s.wait_ge(sm.get(name), sm.v[name])
    return nc, dbg_h





def _in_map(ins, core):
    A = lambda a: np.ascontiguousarray(np.asarray(a, np.float32))
    return {
        "x": A(ins["L0_attn_x"][core]),
        "wq": A(np.asarray(ins["L0_attn_wq"], np.float32).reshape(D, D)),
        "wk": A(np.asarray(ins["L0_attn_wk"], np.float32).reshape(D, D)),
        "wv": A(np.asarray(ins["L0_attn_wv"], np.float32).reshape(D, D)),
        "wo": A(ins["L0_attn_wo"]),
        "omT": A(np.asarray(ins["L0_attn_omega"], np.float32).T),
        "ln1g": A(ins["L0_attn_ln1_g"]), "ln1b": A(ins["L0_attn_ln1_b"]),
        "w1": A(ins["L0_attn_w1"]), "b1": A(ins["L0_attn_b1"]),
        "w2": A(ins["L0_attn_w2"]), "b2": A(ins["L0_attn_b2"]),
        "ln2g": A(ins["L0_attn_ln2_g"]), "ln2b": A(ins["L0_attn_ln2_b"]),
        "Wr": A(ins["L0_ffn_Wr"]), "Wg": A(ins["L0_ffn_Wg"]), "Wu": A(ins["L0_ffn_Wu"]),
        "Wd": A(ins["L0_ffn_Wd"]), "Sg": A(ins["L0_ffn_Sg"]), "Su": A(ins["L0_ffn_Su"]),
        "Sd": A(ins["L0_ffn_Sd"]), "eb": A(ins["L0_ffn_expert_bias"]),
    }


def kernel(**inputs):
    ncores = 8
    nc, _ = build()
    in_maps = [_in_map(inputs, c) for c in range(ncores)]
    res = run_bass_kernel_spmd(nc, in_maps, core_ids=list(range(ncores)))
    return np.stack([np.asarray(res.results[c]["out"], np.float32) for c in range(ncores)], axis=0)
